# version 04
# baseline (speedup 1.0000x reference)
import math
import numpy as np
from contextlib import ExitStack

import concourse.bass as bass
import concourse.mybir as mybir
from concourse.bass_utils import run_bass_kernel_spmd

F32 = mybir.dt.float32
I32 = mybir.dt.int32
ALU = mybir.AluOpType
ACT = mybir.ActivationFunctionType
AXL = mybir.AxisListType

NCORES = 8
TB = 512
NF = 4096
NKT = 32
EPS = 1e-5
TEMP_INV = 10.0
THR = 0.01
GINIT = -1e30
RSQ = 1.0 / math.sqrt(128.0)

ENGS = ("tensor", "vector", "scalar", "sync", "gpsimd")


class Buf:
    __slots__ = ("w", "rs")

    def __init__(self):
        self.w = None
        self.rs = []


class Prog:
    def __init__(self):
        self.ops = {e: [] for e in ENGS}
        self.cnt = {}
        self.floor = {e: {} for e in ENGS}
        self.last_tok = {}

    def emit(self, eng, fn, reads=(), writes=(), waw_ok=False, dma=None, chain=False, extra=()):
        toks = list(extra)
        for b in reads:
            if b.w is not None:
                toks.append(b.w)
        for b in writes:
            if b.w is not None and not waw_ok:
                toks.append(b.w)
            toks.extend(b.rs)
        if dma is not None and chain and dma in self.last_tok:
            toks.append(self.last_tok[dma])
        waits = {}
        for (s, v) in toks:
            if v > waits.get(s, 0):
                waits[s] = v
        fl = self.floor[eng]
        w2 = {}
        for s, v in waits.items():
            if v > fl.get(s, 0):
                w2[s] = v
                fl[s] = v
        sem = dma if dma is not None else eng
        amt = 16 if dma is not None else 1
        self.cnt[sem] = self.cnt.get(sem, 0) + amt
        tok = (sem, self.cnt[sem])
        self.ops[eng].append((fn, w2, sem, amt))
        if dma is not None:
            self.last_tok[dma] = tok
        for b in reads:
            b.rs.append(tok)
        for b in writes:
            b.w = tok
            b.rs = []
        return tok


def build_nc(n_cores=NCORES, nb=4):
    nc = bass.Bass("TRN2", target_bir_lowering=False)
    ntok = nb * TB

    x_in = nc.dram_tensor("x", [ntok, NF], F32, kind="ExternalInput").ap()
    wffn = nc.dram_tensor("wffn", [32, 128, 32, 128], F32, kind="ExternalInput").ap()
    biasc_in = nc.dram_tensor("biasc", [128, 32], F32, kind="ExternalInput").ap()
    asg_in = nc.dram_tensor("asg", [1, NF], I32, kind="ExternalInput").ap()
    cin_in = nc.dram_tensor("cin", [128, 32, 64], F32, kind="ExternalInput").ap()
    wq_in = nc.dram_tensor("wq", [32, 128, 32, 128], F32, kind="ExternalInput").ap()
    bq_in = nc.dram_tensor("bq", [32, 128], F32, kind="ExternalInput").ap()
    wk_in = nc.dram_tensor("wk", [8, 128, 32, 128], F32, kind="ExternalInput").ap()
    bk_in = nc.dram_tensor("bk", [1, 1024], F32, kind="ExternalInput").ap()
    wv_in = nc.dram_tensor("wv", [8, 128, 32, 128], F32, kind="ExternalInput").ap()
    bv_in = nc.dram_tensor("bv", [1, 1024], F32, kind="ExternalInput").ap()
    eye_in = nc.dram_tensor("eye", [128, 128], F32, kind="ExternalInput").ap()
    iota_in = nc.dram_tensor("iota", [64, 1], F32, kind="ExternalInput").ap()
    y_out = nc.dram_tensor("y", [ntok, NF], F32, kind="ExternalOutput").ap()

    xc_s = nc.dram_tensor("xc_s", [nb, 128, NKT * TB], F32).ap()
    x2c_s = nc.dram_tensor("x2c_s", [nb, 128, NKT * TB], F32).ap()
    wqg_s = nc.dram_tensor("wqg_s", [8, 128, 4096], F32).ap()
    stats_s = nc.dram_tensor("stats_s", [16, TB], F32).ap()
    asgf_s = nc.dram_tensor("asgf_s", [1, NF], F32).ap()
    cc_in1 = nc.dram_tensor("cc_in1", [1, 64], F32).ap()
    cc_out1 = nc.dram_tensor("cc_out1", [1, 64], F32).ap()
    cc_in2 = nc.dram_tensor("cc_in2", [1, 64], F32).ap()
    cc_out2 = nc.dram_tensor("cc_out2", [1, 64], F32).ap()

    es = ExitStack()
    sb = lambda name, shape, dt=F32: es.enter_context(nc.sbuf_tensor(name, shape, dt))

    AA = sb("AA", [128, NKT * TB])
    BB = sb("BB", [128, NKT * TB])
    WSa = sb("WSa", [128, 2048])
    WSb = sb("WSb", [128, 2048])
    CIN = sb("CIN", [128, 2048])
    SBC = sb("SBC", [128, TB])
    MBC = sb("MBC", [128, TB])
    MISC = sb("MISC", [128, 2048])
    EYE = sb("EYE", [128, 128])
    MC1 = sb("MC1", [128, 32])
    MC2 = sb("MC2", [128, 32])
    BM1 = sb("BM1", [128, 32])
    BM2 = sb("BM2", [128, 32])
    BIASC = sb("BIASC", [128, 32])
    BQG = sb("BQG", [1, 1024])
    BK = sb("BK", [1, 1024])
    BV = sb("BV", [1, 1024])
    IOTA = sb("IOTA", [64, 1])
    ONESC = sb("ONESC", [128, 1])
    ONES128 = sb("ONES128", [128, 128])
    GPB = sb("GPB", [64, 128])
    ONESR = sb("ONESR", [1, TB])
    INVS = sb("INVS", [1, TB])
    GACC1 = sb("GACC1", [128, 64])
    GACC2 = sb("GACC2", [128, 64])
    SD = sb("SD", [128, 64])
    EV = sb("EV", [128, 64])
    MRG = sb("MRG", [128, 64])
    MX = sb("MX", [128, 1])
    NMX = sb("NMX", [128, 1])
    ZC = sb("ZC", [128, 1])
    ZZ = sb("ZZ", [128, 1])
    SCOL = sb("SCOL", [128, 1])
    M2COL = sb("M2COL", [128, 1])
    GCOL = sb("GCOL", [64, 1])
    GPOS = sb("GPOS", [64, 1])
    GRED = sb("GRED", [1, 64])
    ASGI = sb("ASGI", [1, 2048], I32)

    AOF = 12288
    Qpad = AA[:, AOF + 0:AOF + 128]
    Kpad = AA[:, AOF + 128:AOF + 256]
    QT = [AA[:, AOF + 256 + 128 * i:AOF + 384 + 128 * i] for i in range(2)]
    KT = [AA[:, AOF + 512 + 128 * i:AOF + 640 + 128 * i] for i in range(2)]
    EB = AA[:, AOF + 768:AOF + 1152]
    VB = AA[:, AOF + 1152:AOF + 1368]
    S9 = AA[:, AOF + 1368:AOF + 1584]
    G24 = AA[:, AOF + 1584:AOF + 1776]
    RZ = AA[:, AOF + 1776:AOF + 1800]

    P = [es.enter_context(nc.psum_tensor("P%d" % i, [128, 512], F32)) for i in range(8)]

    bAA = [Buf() for _ in range(NKT)]
    bBB = [Buf() for _ in range(NKT)]
    bWSa, bWSb = Buf(), Buf()
    bCIN = Buf()
    bSBC, bMBC, bMISC = Buf(), Buf(), Buf()
    bEYE = Buf()
    bMC1, bMC2, bBM1, bBM2, bBIASC = Buf(), Buf(), Buf(), Buf(), Buf()
    bBQG, bBK, bBV, bIOTA = Buf(), Buf(), Buf(), Buf()
    bONESC, bONESR, bINVS, bONES128, bGPB = Buf(), Buf(), Buf(), Buf(), Buf()
    bGACC1, bGACC2 = Buf(), Buf()
    bSD, bEV, bMRG, bMX, bNMX, bZC, bZZ = (Buf() for _ in range(7))
    bSCOL, bM2COL, bGCOL, bGPOS, bGRED = (Buf() for _ in range(5))
    bQpad, bKpad = Buf(), Buf()
    bQT = [Buf(), Buf()]
    bKT = [Buf(), Buf()]
    bEB, bVB, bS9, bG24, bRZ = (Buf() for _ in range(5))
    bP = [Buf() for _ in range(8)]
    bxc = [Buf() for _ in range(nb)]
    bx2c = [Buf() for _ in range(nb)]
    bwqg = [Buf() for _ in range(8)]
    bstats = [Buf() for _ in range(16)]
    basgf, bASGI = Buf(), Buf()
    bcc1i, bcc1o, bcc2i, bcc2o = Buf(), Buf(), Buf(), Buf()

    pr = Prog()

    class Rot:
        def __init__(self, a, b):
            self.pair = (a, b)
            self.i = 0

        def take(self):
            j = self.pair[self.i]
            self.i ^= 1
            return j

    SEG_S, SEG_MF, SEG_T, SEG_M = 0, 512, 1024, 1536
    rowS = MISC[0:1, 0:512]
    rowMF = MISC[0:1, 512:1024]
    rowT = MISC[0:1, 1024:1536]
    rowM = MISC[0:1, 1536:2048]

    rotG = Rot(0, 1)
    rotS = Rot(2, 3)
    rotA = Rot(4, 5)
    rotT = Rot(6, 7)

    def dmacp(eng, ch, out_ap, in_ap, reads=(), writes=(), waw_ok=False, accum=False,
              chain=False, extra=()):
        if accum:
            fn = lambda e, o=out_ap, i=in_ap: e.dma_start(out=o, in_=i, accum_op=ALU.add)
        else:
            fn = lambda e, o=out_ap, i=in_ap: e.dma_start(out=o, in_=i)
        return pr.emit(eng, fn, reads=reads, writes=writes, waw_ok=waw_ok, dma=ch,
                       chain=chain, extra=extra)

    def vop(fn, reads=(), writes=(), eng="vector", waw_ok=False):
        return pr.emit(eng, fn, reads=reads, writes=writes, waw_ok=waw_ok)

    def mm(pb, out_ap, lhsT, rhs, start, stop, reads=(), first=False, loose=False):
        fn = lambda e, o=out_ap, l=lhsT, r=rhs, st=start, sp=stop: e.matmul(o, l, r, start=st, stop=sp)
        if first:
            return pr.emit("tensor", fn, reads=reads, writes=(bP[pb],))
        if loose:
            tok = pr.emit("tensor", fn, reads=reads)
            bP[pb].w = tok
            return tok
        return pr.emit("tensor", fn, reads=tuple(reads) + (bP[pb],), writes=(bP[pb],), waw_ok=True)

    def transpose(pb, in_ap, reads):
        fn = lambda e, o=P[pb][:, 0:128], i=in_ap: e.transpose(o, i, EYE[:])
        return pr.emit("tensor", fn, reads=tuple(reads) + (bEYE,), writes=(bP[pb],))

    def bcast_row(dst_sb, dst_buf, src_ap):
        pb = rotT.take()
        mm(pb, P[pb][0:128, 0:TB], ONESR[0:1, 0:128], src_ap, True, True,
           reads=(bONESR, bMISC), first=True)
        vop(lambda e, d=dst_sb, s=P[pb][0:128, 0:TB]: e.tensor_copy(d, s),
            reads=(bP[pb],), writes=(dst_buf,))

    def row_to_col(seg, win, dst_ap, dst_buf):
        pb = rotT.take()
        transpose(pb, MISC[:, seg + 128 * win:seg + 128 * (win + 1)], reads=(bMISC,))
        vop(lambda e, d=dst_ap, s=P[pb][:, 0:1]: e.tensor_copy(d, s),
            reads=(bP[pb],), writes=(dst_buf,))

    def colsum_pass(tile_ap, tile_bufs, out_row, scale, square=False, sq_ap=None, sq_bufs=None,
                    bc_dst=None):
        pb = rotG.take()
        for k in range(NKT):
            if square:
                t = k % 2
                vop(lambda e, d=sq_ap[t], s=tile_ap(k): e.tensor_tensor(d, s, s, op=ALU.mult),
                    reads=(tile_bufs[k],), writes=(sq_bufs[t],))
                mm(pb, P[pb][0:128, 0:TB], ONES128[:], sq_ap[t], k == 0, k == NKT - 1,
                   reads=(bONES128, sq_bufs[t]), first=(k == 0))
            else:
                mm(pb, P[pb][0:128, 0:TB], ONES128[:], tile_ap(k), k == 0, k == NKT - 1,
                   reads=(bONES128, tile_bufs[k]), first=(k == 0))
        vop(lambda e, d=MISC[0:1, out_row:out_row + TB], s=P[pb][0:1, 0:TB], sc=scale:
            e.tensor_scalar(d, s, scalar1=sc, scalar2=None, op0=ALU.mult),
            reads=(bP[pb],), writes=(bMISC,))
        if bc_dst is not None:
            dst, dbuf = bc_dst
            vop(lambda e, d=dst, s=P[pb][0:128, 0:TB], sc=scale:
                e.tensor_scalar(d, s, scalar1=sc, scalar2=None, op0=ALU.mult),
                reads=(bP[pb],), writes=(dbuf,))

    dmacp("gpsimd", "d_msc", EYE[:], eye_in, writes=(bEYE,), chain=True)
    dmacp("gpsimd", "d_msc", IOTA[:], iota_in, writes=(bIOTA,), chain=True)
    dmacp("gpsimd", "d_msc", BIASC[:], biasc_in, writes=(bBIASC,), chain=True)
    dmacp("gpsimd", "d_msc", BK[:], bk_in, writes=(bBK,), chain=True)
    dmacp("gpsimd", "d_msc", BV[:], bv_in, writes=(bBV,), chain=True)

    dmacp("sync", "d_cin", CIN[:], cin_in.rearrange("p k c -> p (k c)"), writes=(bCIN,), chain=True)
    for g in range(8):
        for j in range(4):
            dmacp("gpsimd", "d_msc", BQG[0:1, 128 * g:128 * (g + 1)],
                  bq_in[4 * g + j:4 * g + j + 1, :],
                  writes=(bBQG,), waw_ok=(g > 0 or j > 0), accum=(j > 0), chain=True)
    for h in range(2):
        dst = (WSa if h == 0 else WSb)[0:1, 0:2048]
        bufd = bWSa if h == 0 else bWSb
        dmacp("gpsimd", "d_msc", ASGI[:], asg_in[0:1, 2048 * h:2048 * (h + 1)],
              writes=(bASGI,), chain=True)
        vop(lambda e, d=dst, s=ASGI[:]: e.tensor_copy(d, s),
            reads=(bASGI,), writes=(bufd,))
        dmacp("gpsimd", "d_msc", asgf_s[0:1, 2048 * h:2048 * (h + 1)], dst,
              reads=(bufd,), writes=(basgf,), waw_ok=(h == 1), chain=True)
    vop(lambda e: e.memset(MISC[:], 0.0), writes=(bMISC,))
    vop(lambda e: e.memset(ONESC[:], 1.0), writes=(bONESC,))
    vop(lambda e: e.memset(ONES128[:], 1.0), writes=(bONES128,))
    vop(lambda e: e.memset(ONESR[:], 1.0), writes=(bONESR,))
    vop(lambda e: e.memset(GACC1[:], GINIT), writes=(bGACC1,))
    vop(lambda e: e.memset(GACC2[:], GINIT), writes=(bGACC2,))
    for g in range(8):
        for half in range(2):
            sl = slice(16 * half, 16 * (half + 1))
            dmacp("sync", "d_wqb", WSa[:, :],
                  wq_in[4 * g + 0][:, sl, :].rearrange("p k d -> p (k d)"),
                  writes=(bWSa,), chain=True)
            for j in range(1, 4):
                dmacp("sync", "d_wqb", WSb[:, :],
                      wq_in[4 * g + j][:, sl, :].rearrange("p k d -> p (k d)"),
                      writes=(bWSb,), chain=True)
                vop(lambda e: e.tensor_tensor(WSa[:, :], WSa[:, :], WSb[:, :], op=ALU.add),
                    reads=(bWSa, bWSb), writes=(bWSa,))
            dmacp("sync", "d_wqb", wqg_s[g][:, 2048 * half:2048 * (half + 1)], WSa[:, :],
                  reads=(bWSa,), writes=(bwqg[g],), waw_ok=(half == 1), chain=True)

    arenas = [(AA, bAA), (BB, bBB)]
    for b in range(nb):
        AR, bAR = arenas[b % 2]
        OT, bOT = arenas[1 - b % 2]
        for st in range(4):
            slot = st % 2
            stg = OT[:, 4096 * slot:4096 * (slot + 1)]
            sbufs = tuple(bOT[8 * slot + i] for i in range(8))
            dmacp("gpsimd", "d_stg%d" % slot, stg,
                  x_in[TB * b + 128 * st: TB * b + 128 * (st + 1), :],
                  writes=sbufs, waw_ok=True)
            for k in range(NKT):
                pb = rotT.take()
                transpose(pb, stg[:, 128 * k:128 * (k + 1)], reads=sbufs)
                vop(lambda e, d=AR[:, TB * k + 128 * st: TB * k + 128 * (st + 1)],
                    s=P[pb][:, 0:128]: e.tensor_copy(d, s),
                    reads=(bP[pb],), writes=(bAR[k],), waw_ok=True)
        colsum_pass(lambda k: AR[:, TB * k:TB * (k + 1)], bAR, SEG_MF, 1.0 / NF,
                    bc_dst=(MBC[:], bMBC))
        sq_ap = (OT[:, 8192:8704], OT[:, 8704:9216])
        sq_bufs = (bOT[16], bOT[17])
        colsum_pass(lambda k: AR[:, TB * k:TB * (k + 1)], bAR, SEG_T, 1.0 / NF,
                    square=True, sq_ap=sq_ap, sq_bufs=sq_bufs)
        vop(lambda e: e.tensor_tensor(INVS[:], rowMF, rowMF, op=ALU.mult),
            reads=(bMISC,), writes=(bINVS,))
        vop(lambda e: e.tensor_tensor(rowT, rowT, INVS[:], op=ALU.subtract),
            reads=(bMISC, bINVS), writes=(bMISC,))
        vop(lambda e: e.tensor_scalar(rowT, rowT, scalar1=EPS, scalar2=None, op0=ALU.add),
            reads=(bMISC,), writes=(bMISC,))
        vop(lambda e: e.reciprocal(INVS[:], rowT), reads=(bMISC,), writes=(bINVS,))
        pr.emit("scalar", lambda e: e.activation(rowS, INVS[:], ACT.Sqrt),
                reads=(bINVS,), writes=(bMISC,))
        dmacp("gpsimd", "d_str", stats_s[8 + b:9 + b, :], rowS,
              reads=(bMISC,), writes=(bstats[8 + b],), chain=True)
        dmacp("gpsimd", "d_str", stats_s[0 + b:1 + b, :], rowMF,
              reads=(bMISC,), writes=(bstats[0 + b],), chain=True)
        for k in range(NKT):
            vop(lambda e, t=AR[:, TB * k:TB * (k + 1)]: e.tensor_tensor(t, t, MBC[:], op=ALU.subtract),
                reads=(bAR[k], bMBC), writes=(bAR[k],))

        def dots_block(arena, barena, gacc_ap, bgacc):
            for w in range(4):
                row_to_col(SEG_S, w, SCOL[:], bSCOL)
                pb = rotG.take()
                for k in range(NKT):
                    mm(pb, P[pb][0:128, 0:64], arena[:, TB * k + 128 * w: TB * k + 128 * (w + 1)],
                       CIN[:, 64 * k:64 * (k + 1)], k == 0, k == NKT - 1,
                       reads=(barena[k], bCIN), first=(k == 0))
                vop(lambda e, s=P[pb][0:128, 0:64]: e.tensor_scalar(
                    SD[:], s, scalar1=SCOL[:, 0:1], scalar2=TEMP_INV, op0=ALU.mult, op1=ALU.mult),
                    reads=(bP[pb], bSCOL), writes=(bSD,))
                vop(lambda e: e.reduce_max(MX[:], SD[:], axis=AXL.X), reads=(bSD,), writes=(bMX,))
                vop(lambda e: e.tensor_scalar(NMX[:], MX[:], scalar1=-1.0, scalar2=None, op0=ALU.mult),
                    reads=(bMX,), writes=(bNMX,))
                pr.emit("scalar", lambda e: e.activation(EV[:], SD[:], ACT.Exp, bias=NMX[:, 0:1],
                        accum_out=ZC[:]), reads=(bSD, bNMX), writes=(bEV, bZC))
                vop(lambda e: e.tensor_scalar(ZZ[:], ZC[:], scalar1=THR, scalar2=None, op0=ALU.mult),
                    reads=(bZC,), writes=(bZZ,))
                vop(lambda e: e.tensor_scalar(MRG[:], EV[:], scalar1=ZZ[:, 0:1], scalar2=None,
                    op0=ALU.subtract), reads=(bEV, bZZ), writes=(bMRG,))
                vop(lambda e, g=gacc_ap: e.tensor_tensor(g, g, MRG[:], op=ALU.max),
                    reads=(bMRG, bgacc), writes=(bgacc,))

        dots_block(AR, bAR, GACC1[:], bGACC1)
        for q in range(4):
            dmacp("gpsimd", "d_xcs", xc_s[b][:, 4096 * q:4096 * (q + 1)], AR[:, 4096 * q:4096 * (q + 1)],
                  reads=tuple(bAR[8 * q + i] for i in range(8)), writes=(bxc[b],),
                  waw_ok=(q > 0), chain=True)

    def allreduce(gacc_ap, bgacc, cci, bcci, cco, bcco):
        pr.emit("gpsimd", lambda e, g=gacc_ap: e.tensor_reduce(GRED[:], g, axis=AXL.C, op=ALU.max),
                reads=(bgacc,), writes=(bGRED,))
        dmacp("gpsimd", "d_cc", cci, GRED[:], reads=(bGRED,), writes=(bcci,), chain=True)
        pr.emit("gpsimd", lambda e, i=cci, o=cco: e.collective_compute(
            "AllReduce", ALU.max, replica_groups=[list(range(n_cores))],
            ins=[i.opt()], outs=[o.opt()]),
            reads=(bcci,), writes=(bcco,))

    allreduce(GACC1[:], bGACC1, cc_in1, bcc1i, cc_out1, bcc1o)

    def build_mask(cco, bcco, MC, bMC, BM, bBM):
        dmacp("gpsimd", "d_msc", GCOL[:], cco.rearrange("u c -> c u"),
              reads=(bcco,), writes=(bGCOL,), chain=True)
        vop(lambda e: e.tensor_scalar(GPOS[:], GCOL[:], scalar1=0.0, scalar2=0.5,
            op0=ALU.is_gt, op1=ALU.mult), reads=(bGCOL,), writes=(bGPOS,))
        vop(lambda e: e.tensor_copy(GPB[:], GPOS[:].broadcast_to((64, 128))),
            reads=(bGPOS,), writes=(bGPB,))
        for h in range(2):
            W, bW = (WSa, bWSa) if h == 0 else (WSb, bWSb)
            dmacp("gpsimd", "d_msc", W[0:64, :],
                  asgf_s[0:1, 2048 * h:2048 * (h + 1)].broadcast_to((64, 2048)),
                  reads=(basgf,), writes=(bW,), chain=True)
            vop(lambda e, w=W: e.tensor_scalar(w[0:64, :], w[0:64, :], scalar1=IOTA[:, 0:1],
                scalar2=None, op0=ALU.is_equal), reads=(bW, bIOTA), writes=(bW,))
            for jj in range(4):
                pb = rotG.take()
                mm(pb, P[pb][0:128, 0:512], GPB[:], W[0:64, 512 * jj:512 * (jj + 1)], True, True,
                   reads=(bGPB, bW), first=True)
                vop(lambda e, s=P[pb][0:1, 0:512]: e.tensor_copy(rowT, s),
                    reads=(bP[pb],), writes=(bMISC,))
                for t in range(4):
                    m = 16 * h + 4 * jj + t
                    pb2 = rotT.take()
                    transpose(pb2, MISC[:, SEG_T + 128 * t:SEG_T + 128 * (t + 1)], reads=(bMISC,))
                    vop(lambda e, d=MC[:, m:m + 1], s=P[pb2][:, 0:1]: e.tensor_copy(d, s),
                        reads=(bP[pb2],), writes=(bMC,), waw_ok=True)
        vop(lambda e, d=BM[:], mc=MC[:]: e.tensor_tensor(d, BIASC[:], mc, op=ALU.mult),
            reads=(bBIASC, bMC), writes=(bBM,))

    wq_eng = {"d_w0": "sync", "d_w1": "scalar"}
    wq_names = ["d_w0", "d_w1"]
    wq_i = [0]

    def gemm_slab(srcs, rhs_ap, rhs_bufs, pb, n_out, bias_mm=None):
        slots = ((WSa, bWSa), (WSb, bWSb))
        for h in range(2):
            ch = wq_names[wq_i[0] % 2]
            wq_i[0] += 1
            dmacp(wq_eng[ch], ch, slots[h][0][:, :], srcs[h][0],
                  reads=srcs[h][1], writes=(slots[h][1],))
        for k in range(NKT):
            dst, bufd = slots[k // 16]
            lhs = dst[:, 128 * (k % 16):128 * (k % 16 + 1)]
            mm(pb, P[pb][0:128, 0:n_out], lhs, rhs_ap(k), k == 0,
               (k == NKT - 1) and bias_mm is None,
               reads=(bufd, rhs_bufs[k]), first=(k == 0), loose=True)
        if bias_mm is not None:
            blhs, brhs, brd = bias_mm
            mm(pb, P[pb][0:128, 0:n_out], blhs, brhs, False, True, reads=brd, loose=True)

    def mask_one(m, out_arena, bout, MC, bMC, BM, bBM):
        vop(lambda e, t=out_arena[:, TB * m:TB * (m + 1)], mm_=m, mc=MC, bm2=BM:
            e.tensor_scalar(t, t, scalar1=mc[:, mm_:mm_ + 1], scalar2=bm2[:, mm_:mm_ + 1],
                            op0=ALU.mult, op1=ALU.add),
            reads=(bout[m], bMC, bBM), writes=(bout[m],))

    def ffn(rhs_arena, brhs, out_arena, bout, MC, bMC, BM, bBM, defer_mask):
        for m in range(NKT):
            pb = rotG.take()
            gemm_slab([(wffn[m, :, 0:16, :].rearrange("p k o -> p (k o)"), ()),
                       (wffn[m, :, 16:32, :].rearrange("p k o -> p (k o)"), ())],
                      lambda k: rhs_arena[:, TB * k:TB * (k + 1)], brhs, pb, TB)
            vop(lambda e, d=out_arena[:, TB * m:TB * (m + 1)], s=P[pb][0:128, 0:TB]:
                e.tensor_tensor(d, s, SBC[:], op=ALU.mult),
                reads=(bP[pb], bSBC), writes=(bout[m],))
            if not defer_mask:
                mask_one(m, out_arena, bout, MC, bMC, BM, bBM)

    def resid_add(dst_arena, bdst, src_arena, bsrc):
        for m in range(NKT):
            vop(lambda e, d=dst_arena[:, TB * m:TB * (m + 1)], s=src_arena[:, TB * m:TB * (m + 1)]:
                e.tensor_tensor(d, d, s, op=ALU.add),
                reads=(bdst[m], bsrc[m]), writes=(bdst[m],))

    def ln_stats(arena, barena, sq_ap, sq_bufs, track_M):
        colsum_pass(lambda k: arena[:, TB * k:TB * (k + 1)], barena, SEG_MF, 1.0 / NF,
                    bc_dst=(MBC[:], bMBC))
        colsum_pass(lambda k: arena[:, TB * k:TB * (k + 1)], barena, SEG_T, 1.0 / NF,
                    square=True, sq_ap=sq_ap, sq_bufs=sq_bufs)
        vop(lambda e: e.tensor_tensor(INVS[:], rowMF, rowMF, op=ALU.mult),
            reads=(bMISC,), writes=(bINVS,))
        vop(lambda e: e.tensor_tensor(rowT, rowT, INVS[:], op=ALU.subtract),
            reads=(bMISC, bINVS), writes=(bMISC,))
        vop(lambda e: e.tensor_scalar(rowT, rowT, scalar1=EPS, scalar2=None, op0=ALU.add),
            reads=(bMISC,), writes=(bMISC,))
        vop(lambda e: e.reciprocal(INVS[:], rowT), reads=(bMISC,), writes=(bINVS,))
        pr.emit("scalar", lambda e: e.activation(rowS, INVS[:], ACT.Sqrt),
                reads=(bINVS,), writes=(bMISC,))
        pr.emit("scalar", lambda e: e.activation(INVS[:], rowT, ACT.Sqrt),
                reads=(bMISC, bINVS), writes=(bINVS,))
        if track_M:
            vop(lambda e: e.tensor_tensor(rowM, rowM, rowMF, op=ALU.add),
                reads=(bMISC,), writes=(bMISC,))
        for k in range(NKT):
            vop(lambda e, t=arena[:, TB * k:TB * (k + 1)]: e.tensor_tensor(t, t, MBC[:], op=ALU.subtract),
                reads=(barena[k], bMBC), writes=(barena[k],))

    def dots2_block(arena, barena, gacc_ap, bgacc):
        for w in range(4):
            row_to_col(SEG_S, w, SCOL[:], bSCOL)
            pb = rotG.take()
            for k in range(NKT):
                mm(pb, P[pb][0:128, 0:64], arena[:, TB * k + 128 * w: TB * k + 128 * (w + 1)],
                   CIN[:, 64 * k:64 * (k + 1)], k == 0, k == NKT - 1,
                   reads=(barena[k], bCIN), first=(k == 0))
            vop(lambda e, s=P[pb][0:128, 0:64]: e.tensor_scalar(
                SD[:], s, scalar1=SCOL[:, 0:1], scalar2=TEMP_INV, op0=ALU.mult, op1=ALU.mult),
                reads=(bP[pb], bSCOL), writes=(bSD,))
            vop(lambda e: e.reduce_max(MX[:], SD[:], axis=AXL.X), reads=(bSD,), writes=(bMX,))
            vop(lambda e: e.tensor_scalar(NMX[:], MX[:], scalar1=-1.0, scalar2=None, op0=ALU.mult),
                reads=(bMX,), writes=(bNMX,))
            pr.emit("scalar", lambda e: e.activation(EV[:], SD[:], ACT.Exp, bias=NMX[:, 0:1],
                    accum_out=ZC[:]), reads=(bSD, bNMX), writes=(bEV, bZC))
            vop(lambda e: e.tensor_scalar(ZZ[:], ZC[:], scalar1=THR, scalar2=None, op0=ALU.mult),
                reads=(bZC,), writes=(bZZ,))
            vop(lambda e: e.tensor_scalar(MRG[:], EV[:], scalar1=ZZ[:, 0:1], scalar2=None,
                op0=ALU.subtract), reads=(bEV, bZZ), writes=(bMRG,))
            vop(lambda e, g=gacc_ap: e.tensor_tensor(g, g, MRG[:], op=ALU.max),
                reads=(bMRG, bgacc), writes=(bgacc,))

    for b in range(nb):
        ltok = None
        for q in range(4):
            ltok = dmacp("gpsimd", "d_aal", AA[:, 4096 * q:4096 * (q + 1)],
                         xc_s[b][:, 4096 * q:4096 * (q + 1)],
                         reads=(bxc[b],), writes=tuple(bAA[8 * q + i] for i in range(8)), waw_ok=True)
        for bb_ in bAA:
            bb_.w = ltok
        dmacp("gpsimd", "d_str", rowS, stats_s[8 + b:9 + b, :],
              reads=(bstats[8 + b],), writes=(bMISC,), chain=True)
        bcast_row(SBC[:], bSBC, rowS)
        ffn(AA, bAA, BB, bBB, MC1, bMC1, BM1, bBM1, defer_mask=(b == 0))
        if b == 0:
            build_mask(cc_out1, bcc1o, MC1, bMC1, BM1, bBM1)
            for m in range(NKT):
                mask_one(m, BB, bBB, MC1, bMC1, BM1, bBM1)
        resid_add(BB, bBB, AA, bAA)
        dmacp("gpsimd", "d_str", rowM, stats_s[0 + b:1 + b, :],
              reads=(bstats[0 + b],), writes=(bMISC,), chain=True)
        ln_stats(BB, bBB, (AA[:, 0:512], AA[:, 512:1024]), (bAA[0], bAA[1]), track_M=True)
        bcast_row(SBC[:], bSBC, rowS)
        for (w_src, w_bufs, bias_sb, bias_buf, foff) in (
                (lambda g: wqg_s[g].rearrange("p f -> p f"), bwqg, BQG, bBQG, 0),
                (lambda g: wk_in[g].rearrange("p k d -> p (k d)"), None, BK, bBK, 8),
                (lambda g: wv_in[g].rearrange("p k d -> p (k d)"), None, BV, bBV, 16)):
            for g in range(8):
                pb = rotG.take()
                src = w_src(g)
                rd = (w_bufs[g],) if w_bufs is not None else ()
                gemm_slab([(src[:, 0:2048], rd), (src[:, 2048:4096], rd)],
                          lambda k: BB[:, TB * k:TB * (k + 1)], bBB, pb, TB,
                          bias_mm=(bias_sb[0:1, 128 * g:128 * (g + 1)], INVS[:],
                                   (bias_buf, bINVS)))
                dst = AA[:, TB * (foff + g):TB * (foff + g + 1)]
                vop(lambda e, d=dst, s=P[pb][0:128, 0:TB]: e.tensor_tensor(d, s, SBC[:], op=ALU.mult),
                    reads=(bP[pb], bSBC), writes=(bAA[foff + g],))
        QFb = tuple(bAA[i] for i in range(8))
        KFb = tuple(bAA[8 + i] for i in range(8))
        VFb = tuple(bAA[16 + i] for i in range(8))
        vop(lambda e: e.memset(Qpad, 0.0), writes=(bQpad, bAA[24]))
        vop(lambda e: e.memset(Kpad, 0.0), writes=(bKpad,))
        vop(lambda e: e.memset(VB, 1.0), writes=(bVB,))
        QFr = AA[:, 0:4096].rearrange("p (g t) -> p t g", g=8, t=TB)
        KFr = AA[:, 4096:8192].rearrange("p (g t) -> p t g", g=8, t=TB)
        VFr = AA[:, 8192:12288].rearrange("p (g t) -> p t g", g=8, t=TB)
        triples = [(t0, min(3, TB - t0)) for t0 in range(0, TB, 3)]
        gidx = 0
        while gidx < len(triples):
            grp = triples[gidx:gidx + 8]
            gidx += 8
            T0 = grp[0][0]
            gs = sum(n for _, n in grp)
            vop(lambda e, t0=T0, n=gs: e.tensor_copy(
                VB.rearrange("p (t n) -> p t n", t=24, n=9)[:, 0:n, 0:8],
                VFr[:, t0:t0 + n, :]), reads=VFb, writes=(bVB,))
            for (t0, nr) in grp:
                jloc = t0 - T0
                sl = (t0 // 3) % 2
                vop(lambda e, tt=t0, n=nr: e.tensor_copy(
                    Qpad.rearrange("p (a s) -> p a s", a=4, s=32)[:, 0:n, 0:8],
                    QFr[:, tt:tt + n, :]), reads=QFb, writes=(bQpad,))
                vop(lambda e, tt=t0, n=nr: e.tensor_copy(
                    Kpad.rearrange("p (a s) -> p a s", a=4, s=32)[:, 0:n, 0:8],
                    KFr[:, tt:tt + n, :]), reads=KFb, writes=(bKpad,))
                pbq = rotT.take()
                transpose(pbq, Qpad, reads=(bQpad,))
                vop(lambda e, d=QT[sl][:, :], s=P[pbq][:, 0:128]: e.tensor_copy(d, s),
                    reads=(bP[pbq],), writes=(bQT[sl],))
                pbk = rotT.take()
                transpose(pbk, Kpad, reads=(bKpad,))
                vop(lambda e, d=KT[sl][:, :], s=P[pbk][:, 0:128]: e.tensor_copy(d, s),
                    reads=(bP[pbk],), writes=(bKT[sl],))
                pbs = rotS.take()
                for a in range(nr):
                    mm(pbs, P[pbs][0:128, 128 * a:128 * (a + 1)], KT[sl][32 * a:32 * (a + 1), :],
                       QT[sl][32 * a:32 * (a + 1), :], a == 0, a == nr - 1,
                       reads=(bKT[sl], bQT[sl]), first=(a == 0))
                pr.emit("scalar", lambda e, s=P[pbs][:, 0:128 * nr], d=EB[:, 0:128 * nr]:
                        e.activation(d, s, ACT.Exp, scale=RSQ),
                        reads=(bP[pbs],), writes=(bEB,))
                pba = rotA.take()
                for a in range(nr):
                    j = jloc + a
                    mm(pba, P[pba][0:128, 9 * a:9 * a + 9], EB[:, 128 * a:128 * (a + 1)],
                       VB[:, 9 * j:9 * j + 9], a == 0, a == nr - 1,
                       reads=(bEB, bVB), first=(a == 0))
                vop(lambda e, d=S9[:, 9 * jloc:9 * (jloc + nr)], s=P[pba][:, 0:9 * nr]:
                    e.tensor_copy(d, s), reads=(bP[pba],), writes=(bS9,), waw_ok=True)
            vop(lambda e, n=gs: e.reciprocal(
                RZ.rearrange("p (t u) -> p t u", u=1)[:, 0:n, :],
                S9.rearrange("p (t n) -> p t n", t=24, n=9)[:, 0:n, 8:9]),
                reads=(bS9,), writes=(bRZ,))
            vop(lambda e, n=gs: e.tensor_tensor(
                G24.rearrange("p (t g) -> p t g", t=24, g=8)[:, 0:n, :],
                S9.rearrange("p (t n) -> p t n", t=24, n=9)[:, 0:n, 0:8],
                RZ.rearrange("p (t u) -> p t u", u=1)[:, 0:n, :].broadcast_to((128, n, 8)),
                op=ALU.mult), reads=(bS9, bRZ), writes=(bG24,))
            for hh in range(NKT):
                gg = hh // 4
                bsl = BB[:, TB * hh + T0: TB * hh + T0 + gs].rearrange("p (t u) -> p t u", u=1)
                vop(lambda e, d=bsl, g2=gg, n=gs: e.tensor_tensor(
                    d, d, G24.rearrange("p (t g) -> p t g", t=24, g=8)[:, 0:n, g2:g2 + 1],
                    op=ALU.add), reads=(bBB[hh], bG24), writes=(bBB[hh],))
        ln_stats(BB, bBB, (AA[:, 0:512], AA[:, 512:1024]), (bAA[0], bAA[1]), track_M=True)
        dmacp("gpsimd", "d_str", stats_s[4 + b:5 + b, :], rowS,
              reads=(bMISC,), writes=(bstats[4 + b],), chain=True)
        dmacp("gpsimd", "d_str", stats_s[8 + b:9 + b, :], rowM,
              reads=(bMISC,), writes=(bstats[8 + b],), chain=True)
        dots2_block(BB, bBB, GACC2[:], bGACC2)
        for q in range(4):
            dmacp("gpsimd", "d_xcs", x2c_s[b][:, 4096 * q:4096 * (q + 1)], BB[:, 4096 * q:4096 * (q + 1)],
                  reads=tuple(bBB[8 * q + i] for i in range(8)), writes=(bx2c[b],),
                  waw_ok=(q > 0), chain=True)

    allreduce(GACC2[:], bGACC2, cc_in2, bcc2i, cc_out2, bcc2o)

    for b in range(nb):
        ltok = None
        for q in range(4):
            ltok = dmacp("gpsimd", "d_aal", AA[:, 4096 * q:4096 * (q + 1)],
                         x2c_s[b][:, 4096 * q:4096 * (q + 1)],
                         reads=(bx2c[b],), writes=tuple(bAA[8 * q + i] for i in range(8)), waw_ok=True)
        for bb_ in bAA:
            bb_.w = ltok
        dmacp("gpsimd", "d_str", rowS, stats_s[4 + b:5 + b, :],
              reads=(bstats[4 + b],), writes=(bMISC,), chain=True)
        bcast_row(SBC[:], bSBC, rowS)
        ffn(AA, bAA, BB, bBB, MC2, bMC2, BM2, bBM2, defer_mask=(b == 0))
        if b == 0:
            build_mask(cc_out2, bcc2o, MC2, bMC2, BM2, bBM2)
            for m in range(NKT):
                mask_one(m, BB, bBB, MC2, bMC2, BM2, bBM2)
        resid_add(BB, bBB, AA, bAA)
        dmacp("gpsimd", "d_str", rowM, stats_s[8 + b:9 + b, :],
              reads=(bstats[8 + b],), writes=(bMISC,), chain=True)
        for w in range(4):
            row_to_col(SEG_M, w, M2COL[:], bM2COL)
            for m in range(NKT):
                pb = rotT.take()
                transpose(pb, BB[:, TB * m + 128 * w: TB * m + 128 * (w + 1)], reads=(bBB[m],))
                dst = AA[:, 2048 * (m // 16) + 128 * (m % 16): 2048 * (m // 16) + 128 * (m % 16) + 128]
                db = bAA[m // 4]
                vop(lambda e, d=dst, s=P[pb][:, 0:128]: e.tensor_scalar(
                    d, s, scalar1=M2COL[:, 0:1], scalar2=None, op0=ALU.add),
                    reads=(bP[pb], bM2COL), writes=(db,), waw_ok=True)
            rows0 = TB * b + 128 * w
            dmacp("gpsimd", "d_y0", y_out[rows0:rows0 + 128, 0:2048], AA[:, 0:2048],
                  reads=tuple(bAA[i] for i in range(4)), writes=())
            dmacp("gpsimd", "d_y1", y_out[rows0:rows0 + 128, 2048:4096], AA[:, 2048:4096],
                  reads=tuple(bAA[4 + i] for i in range(4)), writes=())

    semnames = sorted(pr.cnt.keys())
    sems = {s: es.enter_context(nc.semaphore("sem_" + s)) for s in semnames}
    block = es.enter_context(nc.Block())

    def mk(engname):
        def body(eng):
            for (fn, waits, sem, amt) in pr.ops[engname]:
                for s in sorted(waits):
                    eng.wait_ge(sems[s], waits[s])
                r = fn(eng)
                r.then_inc(sems[sem], amt)
            if engname == "scalar":
                eng.wait_ge(sems["d_y0"], pr.cnt["d_y0"])
                eng.wait_ge(sems["d_y1"], pr.cnt["d_y1"])
        return body

    block.tensor(mk("tensor"))
    block.vector(mk("vector"))
    block.scalar(mk("scalar"))
    block.sync(mk("sync"))
    block.gpsimd(mk("gpsimd"))
    es.close()
    return nc


def kernel(L0_attn_x, L0_attn_Wq, L0_attn_bq, L0_attn_Wk, L0_attn_bk, L0_attn_Wv, L0_attn_bv,
           L0_ffn_weight, L0_ffn_bias, L0_ffn_centroids, L0_ffn_assignments):
    x = np.asarray(L0_attn_x, dtype=np.float32)
    B, S, E = x.shape
    ntok_all = B * S
    n_cores = NCORES
    ntok = ntok_all // n_cores
    nb = ntok // TB

    f = np.arange(NF)
    pi = (f % 128) * 32 + (f // 128)

    x2 = np.ascontiguousarray(x.reshape(ntok_all, NF)[:, pi])
    W = np.asarray(L0_ffn_weight, dtype=np.float32)
    Wp = W[np.ix_(pi, pi)]
    wffn = np.ascontiguousarray(Wp.reshape(32, 128, 32, 128).transpose(0, 3, 2, 1))
    biasc = np.ascontiguousarray(np.asarray(L0_ffn_bias, np.float32)[pi].reshape(32, 128).T)
    asg = np.ascontiguousarray(np.asarray(L0_ffn_assignments, np.int32)[pi].reshape(1, NF))
    cin = np.ascontiguousarray(np.asarray(L0_ffn_centroids, np.float32)[:, pi].T
                               .reshape(32, 128, 64).transpose(1, 0, 2))
    Wq = np.asarray(L0_attn_Wq, np.float32)[pi]
    wq = np.ascontiguousarray(Wq.transpose(2, 0, 1).reshape(32, 32, 128, 128).transpose(0, 2, 1, 3))
    bq = np.ascontiguousarray(np.asarray(L0_attn_bq, np.float32).T)
    Wk = np.asarray(L0_attn_Wk, np.float32)[pi]
    wk = np.ascontiguousarray(Wk.transpose(2, 0, 1).reshape(8, 32, 128, 128).transpose(0, 2, 1, 3))
    bk = np.ascontiguousarray(np.asarray(L0_attn_bk, np.float32).T.reshape(1, 1024))
    Wv = np.asarray(L0_attn_Wv, np.float32)[pi]
    wv = np.ascontiguousarray(Wv.transpose(2, 0, 1).reshape(8, 32, 128, 128).transpose(0, 2, 1, 3))
    bv = np.ascontiguousarray(np.asarray(L0_attn_bv, np.float32).T.reshape(1, 1024))
    eye = np.eye(128, dtype=np.float32)
    iota = np.arange(64, dtype=np.float32).reshape(64, 1)

    nc = build_nc(n_cores=n_cores, nb=nb)
    shared = {"wffn": wffn, "biasc": biasc, "asg": asg, "cin": cin, "wq": wq, "bq": bq,
              "wk": wk, "bk": bk, "wv": wv, "bv": bv, "eye": eye, "iota": iota}
    in_maps = []
    for c in range(n_cores):
        d = dict(shared)
        d["x"] = np.ascontiguousarray(x2[ntok * c:ntok * (c + 1)])
        in_maps.append(d)
    res = run_bass_kernel_spmd(nc, in_maps, core_ids=list(range(n_cores)))
    y_int = np.concatenate([np.asarray(res.results[c]["y"]) for c in range(n_cores)], axis=0)
    y = np.empty((ntok_all, NF), dtype=np.float32)
    y[:, pi] = y_int
    return y.reshape(B, S, E)


# version 05
# speedup vs baseline: 1.0137x; 1.0137x over previous
import math
import numpy as np
from contextlib import ExitStack

import concourse.bass as bass
import concourse.mybir as mybir
from concourse.bass_utils import run_bass_kernel_spmd

F32 = mybir.dt.float32
I32 = mybir.dt.int32
ALU = mybir.AluOpType
ACT = mybir.ActivationFunctionType
AXL = mybir.AxisListType

NCORES = 8
TB = 512
NF = 4096
NKT = 32
EPS = 1e-5
TEMP_INV = 10.0
THR = 0.01
GINIT = -1e30
RSQ = 1.0 / math.sqrt(128.0)

ENGS = ("tensor", "vector", "scalar", "sync", "gpsimd")


class Buf:
    __slots__ = ("w", "rs")

    def __init__(self):
        self.w = None
        self.rs = []


class Prog:
    def __init__(self):
        self.ops = {e: [] for e in ENGS}
        self.cnt = {}
        self.floor = {e: {} for e in ENGS}
        self.last_tok = {}

    def emit(self, eng, fn, reads=(), writes=(), waw_ok=False, dma=None, chain=False, extra=()):
        toks = list(extra)
        for b in reads:
            if b.w is not None:
                toks.append(b.w)
        for b in writes:
            if b.w is not None and not waw_ok:
                toks.append(b.w)
            toks.extend(b.rs)
        if dma is not None and chain and dma in self.last_tok:
            toks.append(self.last_tok[dma])
        waits = {}
        for (s, v) in toks:
            if v > waits.get(s, 0):
                waits[s] = v
        fl = self.floor[eng]
        w2 = {}
        for s, v in waits.items():
            if v > fl.get(s, 0):
                w2[s] = v
                fl[s] = v
        sem = dma if dma is not None else eng
        amt = 16 if dma is not None else 1
        self.cnt[sem] = self.cnt.get(sem, 0) + amt
        tok = (sem, self.cnt[sem])
        self.ops[eng].append((fn, w2, sem, amt))
        if dma is not None:
            self.last_tok[dma] = tok
        for b in reads:
            b.rs.append(tok)
        for b in writes:
            b.w = tok
            b.rs = []
        return tok


def build_nc(n_cores=NCORES, nb=4):
    nc = bass.Bass("TRN2", target_bir_lowering=False)
    ntok = nb * TB

    x_in = nc.dram_tensor("x", [ntok, NF], F32, kind="ExternalInput").ap()
    wffn = nc.dram_tensor("wffn", [32, 128, 32, 128], F32, kind="ExternalInput").ap()
    biasc_in = nc.dram_tensor("biasc", [128, 32], F32, kind="ExternalInput").ap()
    asg_in = nc.dram_tensor("asg", [1, NF], I32, kind="ExternalInput").ap()
    cin_in = nc.dram_tensor("cin", [128, 32, 64], F32, kind="ExternalInput").ap()
    wq_in = nc.dram_tensor("wq", [32, 128, 32, 128], F32, kind="ExternalInput").ap()
    bq_in = nc.dram_tensor("bq", [32, 128], F32, kind="ExternalInput").ap()
    wk_in = nc.dram_tensor("wk", [8, 128, 32, 128], F32, kind="ExternalInput").ap()
    bk_in = nc.dram_tensor("bk", [1, 1024], F32, kind="ExternalInput").ap()
    wv_in = nc.dram_tensor("wv", [8, 128, 32, 128], F32, kind="ExternalInput").ap()
    bv_in = nc.dram_tensor("bv", [1, 1024], F32, kind="ExternalInput").ap()
    eye_in = nc.dram_tensor("eye", [128, 128], F32, kind="ExternalInput").ap()
    iota_in = nc.dram_tensor("iota", [64, 1], F32, kind="ExternalInput").ap()
    y_out = nc.dram_tensor("y", [ntok, NF], F32, kind="ExternalOutput").ap()

    xc_s = nc.dram_tensor("xc_s", [nb, 128, NKT * TB], F32).ap()
    x2c_s = nc.dram_tensor("x2c_s", [nb, 128, NKT * TB], F32).ap()
    wqg_s = nc.dram_tensor("wqg_s", [8, 128, 4096], F32).ap()
    stats_s = nc.dram_tensor("stats_s", [16, TB], F32).ap()
    asgf_s = nc.dram_tensor("asgf_s", [1, NF], F32).ap()
    cc_in1 = nc.dram_tensor("cc_in1", [1, 64], F32).ap()
    cc_out1 = nc.dram_tensor("cc_out1", [1, 64], F32).ap()
    cc_in2 = nc.dram_tensor("cc_in2", [1, 64], F32).ap()
    cc_out2 = nc.dram_tensor("cc_out2", [1, 64], F32).ap()

    es = ExitStack()
    sb = lambda name, shape, dt=F32: es.enter_context(nc.sbuf_tensor(name, shape, dt))

    AA = sb("AA", [128, NKT * TB])
    BB = sb("BB", [128, NKT * TB])
    WSa = sb("WSa", [128, 2048])
    WSb = sb("WSb", [128, 2048])
    CIN = sb("CIN", [128, 2048])
    SBC = sb("SBC", [128, TB])
    MBC = sb("MBC", [128, TB])
    MISC = sb("MISC", [128, 2048])
    EYE = sb("EYE", [128, 128])
    MC1 = sb("MC1", [128, 32])
    MC2 = sb("MC2", [128, 32])
    BM1 = sb("BM1", [128, 32])
    BM2 = sb("BM2", [128, 32])
    BIASC = sb("BIASC", [128, 32])
    BQG = sb("BQG", [1, 1024])
    BK = sb("BK", [1, 1024])
    BV = sb("BV", [1, 1024])
    IOTA = sb("IOTA", [64, 1])
    ONESC = sb("ONESC", [128, 1])
    ONES128 = sb("ONES128", [128, 128])
    GPB = sb("GPB", [64, 128])
    ONESR = sb("ONESR", [1, TB])
    INVS = sb("INVS", [1, TB])
    GACC1 = sb("GACC1", [128, 64])
    GACC2 = sb("GACC2", [128, 64])
    SD = sb("SD", [128, 64])
    EV = sb("EV", [128, 64])
    MRG = sb("MRG", [128, 64])
    MX = sb("MX", [128, 1])
    NMX = sb("NMX", [128, 1])
    ZC = sb("ZC", [128, 1])
    ZZ = sb("ZZ", [128, 1])
    SCOL = sb("SCOL", [128, 1])
    M2COL = sb("M2COL", [128, 1])
    GCOL = sb("GCOL", [64, 1])
    GPOS = sb("GPOS", [64, 1])
    GRED = sb("GRED", [1, 64])
    ASGI = sb("ASGI", [1, 2048], I32)

    AOF = 12288
    Qpad = AA[:, AOF + 0:AOF + 128]
    Kpad = AA[:, AOF + 128:AOF + 256]
    QT = [AA[:, AOF + 256 + 128 * i:AOF + 384 + 128 * i] for i in range(2)]
    KT = [AA[:, AOF + 512 + 128 * i:AOF + 640 + 128 * i] for i in range(2)]
    EB = AA[:, AOF + 768:AOF + 1152]
    VB = AA[:, AOF + 1152:AOF + 1368]
    S9 = AA[:, AOF + 1368:AOF + 1584]
    G24 = AA[:, AOF + 1584:AOF + 1776]
    RZ = AA[:, AOF + 1776:AOF + 1800]

    P = [es.enter_context(nc.psum_tensor("P%d" % i, [128, 512], F32)) for i in range(8)]

    bAA = [Buf() for _ in range(NKT)]
    bBB = [Buf() for _ in range(NKT)]
    bWSa, bWSb = Buf(), Buf()
    bCIN = Buf()
    bSBC, bMBC, bMISC = Buf(), Buf(), Buf()
    bEYE = Buf()
    bMC1, bMC2, bBM1, bBM2, bBIASC = Buf(), Buf(), Buf(), Buf(), Buf()
    bBQG, bBK, bBV, bIOTA = Buf(), Buf(), Buf(), Buf()
    bONESC, bONESR, bINVS, bONES128, bGPB = Buf(), Buf(), Buf(), Buf(), Buf()
    bGACC1, bGACC2 = Buf(), Buf()
    bSD, bEV, bMRG, bMX, bNMX, bZC, bZZ = (Buf() for _ in range(7))
    bSCOL, bM2COL, bGCOL, bGPOS, bGRED = (Buf() for _ in range(5))
    bQpad, bKpad = Buf(), Buf()
    bQT = [Buf(), Buf()]
    bKT = [Buf(), Buf()]
    bEB, bVB, bS9, bG24, bRZ = (Buf() for _ in range(5))
    bP = [Buf() for _ in range(8)]
    bxc = [Buf() for _ in range(nb)]
    bx2c = [Buf() for _ in range(nb)]
    bwqg = [Buf() for _ in range(8)]
    bstats = [Buf() for _ in range(16)]
    basgf, bASGI = Buf(), Buf()
    bcc1i, bcc1o, bcc2i, bcc2o = Buf(), Buf(), Buf(), Buf()

    pr = Prog()

    class Rot:
        def __init__(self, a, b):
            self.pair = (a, b)
            self.i = 0

        def take(self):
            j = self.pair[self.i]
            self.i ^= 1
            return j

    SEG_S, SEG_MF, SEG_T, SEG_M = 0, 512, 1024, 1536
    rowS = MISC[0:1, 0:512]
    rowMF = MISC[0:1, 512:1024]
    rowT = MISC[0:1, 1024:1536]
    rowM = MISC[0:1, 1536:2048]

    rotG = Rot(0, 1)
    rotS = Rot(2, 3)
    rotA = Rot(4, 5)
    rotT = Rot(6, 7)

    def dmacp(eng, ch, out_ap, in_ap, reads=(), writes=(), waw_ok=False, accum=False,
              chain=False, extra=()):
        if accum:
            fn = lambda e, o=out_ap, i=in_ap: e.dma_start(out=o, in_=i, accum_op=ALU.add)
        else:
            fn = lambda e, o=out_ap, i=in_ap: e.dma_start(out=o, in_=i)
        return pr.emit(eng, fn, reads=reads, writes=writes, waw_ok=waw_ok, dma=ch,
                       chain=chain, extra=extra)

    def vop(fn, reads=(), writes=(), eng="vector", waw_ok=False):
        return pr.emit(eng, fn, reads=reads, writes=writes, waw_ok=waw_ok)

    def mm(pb, out_ap, lhsT, rhs, start, stop, reads=(), first=False, loose=False):
        fn = lambda e, o=out_ap, l=lhsT, r=rhs, st=start, sp=stop: e.matmul(o, l, r, start=st, stop=sp)
        if first:
            return pr.emit("tensor", fn, reads=reads, writes=(bP[pb],))
        if loose:
            tok = pr.emit("tensor", fn, reads=reads)
            bP[pb].w = tok
            return tok
        return pr.emit("tensor", fn, reads=tuple(reads) + (bP[pb],), writes=(bP[pb],), waw_ok=True)

    def transpose(pb, in_ap, reads):
        fn = lambda e, o=P[pb][:, 0:128], i=in_ap: e.transpose(o, i, EYE[:])
        return pr.emit("tensor", fn, reads=tuple(reads) + (bEYE,), writes=(bP[pb],))

    def bcast_row(dst_sb, dst_buf, src_ap):
        pb = rotT.take()
        mm(pb, P[pb][0:128, 0:TB], ONESR[0:1, 0:128], src_ap, True, True,
           reads=(bONESR, bMISC), first=True)
        vop(lambda e, d=dst_sb, s=P[pb][0:128, 0:TB]: e.tensor_copy(d, s),
            reads=(bP[pb],), writes=(dst_buf,))

    def row_to_col(seg, win, dst_ap, dst_buf):
        pb = rotT.take()
        transpose(pb, MISC[:, seg + 128 * win:seg + 128 * (win + 1)], reads=(bMISC,))
        vop(lambda e, d=dst_ap, s=P[pb][:, 0:1]: e.tensor_copy(d, s),
            reads=(bP[pb],), writes=(dst_buf,))

    def colsum_pass(tile_ap, tile_bufs, out_row, scale, square=False, sq_ap=None, sq_bufs=None,
                    bc_dst=None):
        pb = rotG.take()
        for k in range(NKT):
            if square:
                t = k % 2
                vop(lambda e, d=sq_ap[t], s=tile_ap(k): e.tensor_tensor(d, s, s, op=ALU.mult),
                    reads=(tile_bufs[k],), writes=(sq_bufs[t],))
                mm(pb, P[pb][0:128, 0:TB], ONES128[:], sq_ap[t], k == 0, k == NKT - 1,
                   reads=(bONES128, sq_bufs[t]), first=(k == 0))
            else:
                mm(pb, P[pb][0:128, 0:TB], ONES128[:], tile_ap(k), k == 0, k == NKT - 1,
                   reads=(bONES128, tile_bufs[k]), first=(k == 0))
        vop(lambda e, d=MISC[0:1, out_row:out_row + TB], s=P[pb][0:1, 0:TB], sc=scale:
            e.tensor_scalar(d, s, scalar1=sc, scalar2=None, op0=ALU.mult),
            reads=(bP[pb],), writes=(bMISC,))
        if bc_dst is not None:
            dst, dbuf = bc_dst
            vop(lambda e, d=dst, s=P[pb][0:128, 0:TB], sc=scale:
                e.tensor_scalar(d, s, scalar1=sc, scalar2=None, op0=ALU.mult),
                reads=(bP[pb],), writes=(dbuf,))

    dmacp("gpsimd", "d_msc", EYE[:], eye_in, writes=(bEYE,), chain=True)
    dmacp("gpsimd", "d_msc", IOTA[:], iota_in, writes=(bIOTA,), chain=True)
    dmacp("gpsimd", "d_msc", BIASC[:], biasc_in, writes=(bBIASC,), chain=True)
    dmacp("gpsimd", "d_msc", BK[:], bk_in, writes=(bBK,), chain=True)
    dmacp("gpsimd", "d_msc", BV[:], bv_in, writes=(bBV,), chain=True)

    dmacp("sync", "d_cin", CIN[:], cin_in.rearrange("p k c -> p (k c)"), writes=(bCIN,), chain=True)
    for g in range(8):
        for j in range(4):
            dmacp("gpsimd", "d_msc", BQG[0:1, 128 * g:128 * (g + 1)],
                  bq_in[4 * g + j:4 * g + j + 1, :],
                  writes=(bBQG,), waw_ok=(g > 0 or j > 0), accum=(j > 0), chain=True)
    for h in range(2):
        dst = (WSa if h == 0 else WSb)[0:1, 0:2048]
        bufd = bWSa if h == 0 else bWSb
        dmacp("gpsimd", "d_msc", ASGI[:], asg_in[0:1, 2048 * h:2048 * (h + 1)],
              writes=(bASGI,), chain=True)
        vop(lambda e, d=dst, s=ASGI[:]: e.tensor_copy(d, s),
            reads=(bASGI,), writes=(bufd,))
        dmacp("gpsimd", "d_msc", asgf_s[0:1, 2048 * h:2048 * (h + 1)], dst,
              reads=(bufd,), writes=(basgf,), waw_ok=(h == 1), chain=True)
    vop(lambda e: e.memset(MISC[:], 0.0), writes=(bMISC,))
    vop(lambda e: e.memset(ONESC[:], 1.0), writes=(bONESC,))
    vop(lambda e: e.memset(ONES128[:], 1.0), writes=(bONES128,))
    vop(lambda e: e.memset(ONESR[:], 1.0), writes=(bONESR,))
    vop(lambda e: e.memset(GACC1[:], GINIT), writes=(bGACC1,))
    vop(lambda e: e.memset(GACC2[:], GINIT), writes=(bGACC2,))
    for g in range(8):
        for half in range(2):
            sl = slice(16 * half, 16 * (half + 1))
            dmacp("sync", "d_wqb", WSa[:, :],
                  wq_in[4 * g + 0][:, sl, :].rearrange("p k d -> p (k d)"),
                  writes=(bWSa,), chain=True)
            for j in range(1, 4):
                dmacp("sync", "d_wqb", WSb[:, :],
                      wq_in[4 * g + j][:, sl, :].rearrange("p k d -> p (k d)"),
                      writes=(bWSb,), chain=True)
                vop(lambda e: e.tensor_tensor(WSa[:, :], WSa[:, :], WSb[:, :], op=ALU.add),
                    reads=(bWSa, bWSb), writes=(bWSa,))
            dmacp("sync", "d_wqb", wqg_s[g][:, 2048 * half:2048 * (half + 1)], WSa[:, :],
                  reads=(bWSa,), writes=(bwqg[g],), waw_ok=(half == 1), chain=True)

    arenas = [(AA, bAA), (BB, bBB)]
    for b in range(nb):
        AR, bAR = arenas[b % 2]
        OT, bOT = arenas[1 - b % 2]
        for st in range(4):
            slot = st % 2
            stg = OT[:, 4096 * slot:4096 * (slot + 1)]
            sbufs = tuple(bOT[8 * slot + i] for i in range(8))
            dmacp("gpsimd", "d_stg%d" % slot, stg,
                  x_in[TB * b + 128 * st: TB * b + 128 * (st + 1), :],
                  writes=sbufs, waw_ok=True)
            for k in range(NKT):
                pb = rotT.take()
                transpose(pb, stg[:, 128 * k:128 * (k + 1)], reads=sbufs)
                vop(lambda e, d=AR[:, TB * k + 128 * st: TB * k + 128 * (st + 1)],
                    s=P[pb][:, 0:128]: e.tensor_copy(d, s),
                    reads=(bP[pb],), writes=(bAR[k],), waw_ok=True)
        colsum_pass(lambda k: AR[:, TB * k:TB * (k + 1)], bAR, SEG_MF, 1.0 / NF,
                    bc_dst=(MBC[:], bMBC))
        sq_ap = (OT[:, 8192:8704], OT[:, 8704:9216])
        sq_bufs = (bOT[16], bOT[17])
        colsum_pass(lambda k: AR[:, TB * k:TB * (k + 1)], bAR, SEG_T, 1.0 / NF,
                    square=True, sq_ap=sq_ap, sq_bufs=sq_bufs)
        vop(lambda e: e.tensor_tensor(INVS[:], rowMF, rowMF, op=ALU.mult),
            reads=(bMISC,), writes=(bINVS,))
        vop(lambda e: e.tensor_tensor(rowT, rowT, INVS[:], op=ALU.subtract),
            reads=(bMISC, bINVS), writes=(bMISC,))
        vop(lambda e: e.tensor_scalar(rowT, rowT, scalar1=EPS, scalar2=None, op0=ALU.add),
            reads=(bMISC,), writes=(bMISC,))
        vop(lambda e: e.reciprocal(INVS[:], rowT), reads=(bMISC,), writes=(bINVS,))
        pr.emit("scalar", lambda e: e.activation(rowS, INVS[:], ACT.Sqrt),
                reads=(bINVS,), writes=(bMISC,))
        dmacp("gpsimd", "d_str", stats_s[8 + b:9 + b, :], rowS,
              reads=(bMISC,), writes=(bstats[8 + b],), chain=True)
        dmacp("gpsimd", "d_str", stats_s[0 + b:1 + b, :], rowMF,
              reads=(bMISC,), writes=(bstats[0 + b],), chain=True)
        for k in range(NKT):
            vop(lambda e, t=AR[:, TB * k:TB * (k + 1)]: e.tensor_tensor(t, t, MBC[:], op=ALU.subtract),
                reads=(bAR[k], bMBC), writes=(bAR[k],))

        def dots_block(arena, barena, gacc_ap, bgacc):
            for w in range(4):
                row_to_col(SEG_S, w, SCOL[:], bSCOL)
                pb = rotG.take()
                for k in range(NKT):
                    mm(pb, P[pb][0:128, 0:64], arena[:, TB * k + 128 * w: TB * k + 128 * (w + 1)],
                       CIN[:, 64 * k:64 * (k + 1)], k == 0, k == NKT - 1,
                       reads=(barena[k], bCIN), first=(k == 0), loose=True)
                vop(lambda e, s=P[pb][0:128, 0:64]: e.tensor_scalar(
                    SD[:], s, scalar1=SCOL[:, 0:1], scalar2=TEMP_INV, op0=ALU.mult, op1=ALU.mult),
                    reads=(bP[pb], bSCOL), writes=(bSD,))
                vop(lambda e: e.reduce_max(MX[:], SD[:], axis=AXL.X), reads=(bSD,), writes=(bMX,))
                vop(lambda e: e.tensor_scalar(NMX[:], MX[:], scalar1=-1.0, scalar2=None, op0=ALU.mult),
                    reads=(bMX,), writes=(bNMX,))
                pr.emit("scalar", lambda e: e.activation(EV[:], SD[:], ACT.Exp, bias=NMX[:, 0:1],
                        accum_out=ZC[:]), reads=(bSD, bNMX), writes=(bEV, bZC))
                vop(lambda e: e.tensor_scalar(ZZ[:], ZC[:], scalar1=THR, scalar2=None, op0=ALU.mult),
                    reads=(bZC,), writes=(bZZ,))
                vop(lambda e: e.tensor_scalar(MRG[:], EV[:], scalar1=ZZ[:, 0:1], scalar2=None,
                    op0=ALU.subtract), reads=(bEV, bZZ), writes=(bMRG,))
                vop(lambda e, g=gacc_ap: e.tensor_tensor(g, g, MRG[:], op=ALU.max),
                    reads=(bMRG, bgacc), writes=(bgacc,))

        dots_block(AR, bAR, GACC1[:], bGACC1)
        for q in range(4):
            dmacp("gpsimd", "d_xcs", xc_s[b][:, 4096 * q:4096 * (q + 1)], AR[:, 4096 * q:4096 * (q + 1)],
                  reads=tuple(bAR[8 * q + i] for i in range(8)), writes=(bxc[b],),
                  waw_ok=(q > 0), chain=True)

    def allreduce(gacc_ap, bgacc, cci, bcci, cco, bcco):
        pr.emit("gpsimd", lambda e, g=gacc_ap: e.tensor_reduce(GRED[:], g, axis=AXL.C, op=ALU.max),
                reads=(bgacc,), writes=(bGRED,))
        dmacp("gpsimd", "d_cc", cci, GRED[:], reads=(bGRED,), writes=(bcci,), chain=True)
        pr.emit("gpsimd", lambda e, i=cci, o=cco: e.collective_compute(
            "AllReduce", ALU.max, replica_groups=[list(range(n_cores))],
            ins=[i.opt()], outs=[o.opt()]),
            reads=(bcci,), writes=(bcco,))

    allreduce(GACC1[:], bGACC1, cc_in1, bcc1i, cc_out1, bcc1o)

    def build_mask(cco, bcco, MC, bMC, BM, bBM):
        dmacp("gpsimd", "d_msc", GCOL[:], cco.rearrange("u c -> c u"),
              reads=(bcco,), writes=(bGCOL,), chain=True)
        vop(lambda e: e.tensor_scalar(GPOS[:], GCOL[:], scalar1=0.0, scalar2=0.5,
            op0=ALU.is_gt, op1=ALU.mult), reads=(bGCOL,), writes=(bGPOS,))
        vop(lambda e: e.tensor_copy(GPB[:], GPOS[:].broadcast_to((64, 128))),
            reads=(bGPOS,), writes=(bGPB,))
        for h in range(2):
            W, bW = (WSa, bWSa) if h == 0 else (WSb, bWSb)
            dmacp("gpsimd", "d_msc", W[0:64, :],
                  asgf_s[0:1, 2048 * h:2048 * (h + 1)].broadcast_to((64, 2048)),
                  reads=(basgf,), writes=(bW,), chain=True)
            vop(lambda e, w=W: e.tensor_scalar(w[0:64, :], w[0:64, :], scalar1=IOTA[:, 0:1],
                scalar2=None, op0=ALU.is_equal), reads=(bW, bIOTA), writes=(bW,))
            for jj in range(4):
                pb = rotG.take()
                mm(pb, P[pb][0:128, 0:512], GPB[:], W[0:64, 512 * jj:512 * (jj + 1)], True, True,
                   reads=(bGPB, bW), first=True)
                vop(lambda e, s=P[pb][0:1, 0:512]: e.tensor_copy(rowT, s),
                    reads=(bP[pb],), writes=(bMISC,))
                for t in range(4):
                    m = 16 * h + 4 * jj + t
                    pb2 = rotT.take()
                    transpose(pb2, MISC[:, SEG_T + 128 * t:SEG_T + 128 * (t + 1)], reads=(bMISC,))
                    vop(lambda e, d=MC[:, m:m + 1], s=P[pb2][:, 0:1]: e.tensor_copy(d, s),
                        reads=(bP[pb2],), writes=(bMC,), waw_ok=True)
        vop(lambda e, d=BM[:], mc=MC[:]: e.tensor_tensor(d, BIASC[:], mc, op=ALU.mult),
            reads=(bBIASC, bMC), writes=(bBM,))

    wq_eng = {"d_w0": "sync", "d_w1": "scalar"}
    wq_names = ["d_w0", "d_w1"]
    wq_i = [0]

    def gemm_slab(srcs, rhs_ap, rhs_bufs, pb, n_out, bias_mm=None):
        slots = ((WSa, bWSa), (WSb, bWSb))
        for h in range(2):
            ch = wq_names[wq_i[0] % 2]
            wq_i[0] += 1
            dmacp(wq_eng[ch], ch, slots[h][0][:, :], srcs[h][0],
                  reads=srcs[h][1], writes=(slots[h][1],))
        for k in range(NKT):
            dst, bufd = slots[k // 16]
            lhs = dst[:, 128 * (k % 16):128 * (k % 16 + 1)]
            mm(pb, P[pb][0:128, 0:n_out], lhs, rhs_ap(k), k == 0,
               (k == NKT - 1) and bias_mm is None,
               reads=(bufd, rhs_bufs[k]), first=(k == 0), loose=True)
        if bias_mm is not None:
            blhs, brhs, brd = bias_mm
            mm(pb, P[pb][0:128, 0:n_out], blhs, brhs, False, True, reads=brd, loose=True)

    def mask_one(m, out_arena, bout, MC, bMC, BM, bBM):
        vop(lambda e, t=out_arena[:, TB * m:TB * (m + 1)], mm_=m, mc=MC, bm2=BM:
            e.tensor_scalar(t, t, scalar1=mc[:, mm_:mm_ + 1], scalar2=bm2[:, mm_:mm_ + 1],
                            op0=ALU.mult, op1=ALU.add),
            reads=(bout[m], bMC, bBM), writes=(bout[m],))

    def ffn(rhs_arena, brhs, out_arena, bout, MC, bMC, BM, bBM, defer_mask):
        for m in range(NKT):
            pb = rotG.take()
            gemm_slab([(wffn[m, :, 0:16, :].rearrange("p k o -> p (k o)"), ()),
                       (wffn[m, :, 16:32, :].rearrange("p k o -> p (k o)"), ())],
                      lambda k: rhs_arena[:, TB * k:TB * (k + 1)], brhs, pb, TB)
            vop(lambda e, d=out_arena[:, TB * m:TB * (m + 1)], s=P[pb][0:128, 0:TB]:
                e.tensor_tensor(d, s, SBC[:], op=ALU.mult),
                reads=(bP[pb], bSBC), writes=(bout[m],))
            if not defer_mask:
                mask_one(m, out_arena, bout, MC, bMC, BM, bBM)

    def resid_add(dst_arena, bdst, src_arena, bsrc):
        for m in range(NKT):
            vop(lambda e, d=dst_arena[:, TB * m:TB * (m + 1)], s=src_arena[:, TB * m:TB * (m + 1)]:
                e.tensor_tensor(d, d, s, op=ALU.add),
                reads=(bdst[m], bsrc[m]), writes=(bdst[m],))

    def ln_stats(arena, barena, sq_ap, sq_bufs, track_M):
        colsum_pass(lambda k: arena[:, TB * k:TB * (k + 1)], barena, SEG_MF, 1.0 / NF,
                    bc_dst=(MBC[:], bMBC))
        colsum_pass(lambda k: arena[:, TB * k:TB * (k + 1)], barena, SEG_T, 1.0 / NF,
                    square=True, sq_ap=sq_ap, sq_bufs=sq_bufs)
        vop(lambda e: e.tensor_tensor(INVS[:], rowMF, rowMF, op=ALU.mult),
            reads=(bMISC,), writes=(bINVS,))
        vop(lambda e: e.tensor_tensor(rowT, rowT, INVS[:], op=ALU.subtract),
            reads=(bMISC, bINVS), writes=(bMISC,))
        vop(lambda e: e.tensor_scalar(rowT, rowT, scalar1=EPS, scalar2=None, op0=ALU.add),
            reads=(bMISC,), writes=(bMISC,))
        vop(lambda e: e.reciprocal(INVS[:], rowT), reads=(bMISC,), writes=(bINVS,))
        pr.emit("scalar", lambda e: e.activation(rowS, INVS[:], ACT.Sqrt),
                reads=(bINVS,), writes=(bMISC,))
        pr.emit("scalar", lambda e: e.activation(INVS[:], rowT, ACT.Sqrt),
                reads=(bMISC, bINVS), writes=(bINVS,))
        if track_M:
            vop(lambda e: e.tensor_tensor(rowM, rowM, rowMF, op=ALU.add),
                reads=(bMISC,), writes=(bMISC,))
        for k in range(NKT):
            vop(lambda e, t=arena[:, TB * k:TB * (k + 1)]: e.tensor_tensor(t, t, MBC[:], op=ALU.subtract),
                reads=(barena[k], bMBC), writes=(barena[k],))

    def dots2_block(arena, barena, gacc_ap, bgacc):
        for w in range(4):
            row_to_col(SEG_S, w, SCOL[:], bSCOL)
            pb = rotG.take()
            for k in range(NKT):
                mm(pb, P[pb][0:128, 0:64], arena[:, TB * k + 128 * w: TB * k + 128 * (w + 1)],
                   CIN[:, 64 * k:64 * (k + 1)], k == 0, k == NKT - 1,
                   reads=(barena[k], bCIN), first=(k == 0), loose=True)
            vop(lambda e, s=P[pb][0:128, 0:64]: e.tensor_scalar(
                SD[:], s, scalar1=SCOL[:, 0:1], scalar2=TEMP_INV, op0=ALU.mult, op1=ALU.mult),
                reads=(bP[pb], bSCOL), writes=(bSD,))
            vop(lambda e: e.reduce_max(MX[:], SD[:], axis=AXL.X), reads=(bSD,), writes=(bMX,))
            vop(lambda e: e.tensor_scalar(NMX[:], MX[:], scalar1=-1.0, scalar2=None, op0=ALU.mult),
                reads=(bMX,), writes=(bNMX,))
            pr.emit("scalar", lambda e: e.activation(EV[:], SD[:], ACT.Exp, bias=NMX[:, 0:1],
                    accum_out=ZC[:]), reads=(bSD, bNMX), writes=(bEV, bZC))
            vop(lambda e: e.tensor_scalar(ZZ[:], ZC[:], scalar1=THR, scalar2=None, op0=ALU.mult),
                reads=(bZC,), writes=(bZZ,))
            vop(lambda e: e.tensor_scalar(MRG[:], EV[:], scalar1=ZZ[:, 0:1], scalar2=None,
                op0=ALU.subtract), reads=(bEV, bZZ), writes=(bMRG,))
            vop(lambda e, g=gacc_ap: e.tensor_tensor(g, g, MRG[:], op=ALU.max),
                reads=(bMRG, bgacc), writes=(bgacc,))

    for b in range(nb):
        ltok = None
        for q in range(4):
            ltok = dmacp("gpsimd", "d_aal", AA[:, 4096 * q:4096 * (q + 1)],
                         xc_s[b][:, 4096 * q:4096 * (q + 1)],
                         reads=(bxc[b],), writes=tuple(bAA[8 * q + i] for i in range(8)), waw_ok=True)
        for bb_ in bAA:
            bb_.w = ltok
        dmacp("gpsimd", "d_str", rowS, stats_s[8 + b:9 + b, :],
              reads=(bstats[8 + b],), writes=(bMISC,), chain=True)
        bcast_row(SBC[:], bSBC, rowS)
        ffn(AA, bAA, BB, bBB, MC1, bMC1, BM1, bBM1, defer_mask=(b == 0))
        if b == 0:
            build_mask(cc_out1, bcc1o, MC1, bMC1, BM1, bBM1)
            for m in range(NKT):
                mask_one(m, BB, bBB, MC1, bMC1, BM1, bBM1)
        resid_add(BB, bBB, AA, bAA)
        dmacp("gpsimd", "d_str", rowM, stats_s[0 + b:1 + b, :],
              reads=(bstats[0 + b],), writes=(bMISC,), chain=True)
        ln_stats(BB, bBB, (AA[:, 0:512], AA[:, 512:1024]), (bAA[0], bAA[1]), track_M=True)
        bcast_row(SBC[:], bSBC, rowS)
        for (w_src, w_bufs, bias_sb, bias_buf, foff) in (
                (lambda g: wqg_s[g].rearrange("p f -> p f"), bwqg, BQG, bBQG, 0),
                (lambda g: wk_in[g].rearrange("p k d -> p (k d)"), None, BK, bBK, 8),
                (lambda g: wv_in[g].rearrange("p k d -> p (k d)"), None, BV, bBV, 16)):
            for g in range(8):
                pb = rotG.take()
                src = w_src(g)
                rd = (w_bufs[g],) if w_bufs is not None else ()
                gemm_slab([(src[:, 0:2048], rd), (src[:, 2048:4096], rd)],
                          lambda k: BB[:, TB * k:TB * (k + 1)], bBB, pb, TB,
                          bias_mm=(bias_sb[0:1, 128 * g:128 * (g + 1)], INVS[:],
                                   (bias_buf, bINVS)))
                dst = AA[:, TB * (foff + g):TB * (foff + g + 1)]
                vop(lambda e, d=dst, s=P[pb][0:128, 0:TB]: e.tensor_tensor(d, s, SBC[:], op=ALU.mult),
                    reads=(bP[pb], bSBC), writes=(bAA[foff + g],))
        QFb = tuple(bAA[i] for i in range(8))
        KFb = tuple(bAA[8 + i] for i in range(8))
        VFb = tuple(bAA[16 + i] for i in range(8))
        vop(lambda e: e.memset(Qpad, 0.0), writes=(bQpad, bAA[24]))
        vop(lambda e: e.memset(Kpad, 0.0), writes=(bKpad,))
        vop(lambda e: e.memset(VB, 1.0), writes=(bVB,))
        QFr = AA[:, 0:4096].rearrange("p (g t) -> p t g", g=8, t=TB)
        KFr = AA[:, 4096:8192].rearrange("p (g t) -> p t g", g=8, t=TB)
        VFr = AA[:, 8192:12288].rearrange("p (g t) -> p t g", g=8, t=TB)
        triples = [(t0, min(3, TB - t0)) for t0 in range(0, TB, 3)]
        gidx = 0
        while gidx < len(triples):
            grp = triples[gidx:gidx + 8]
            gidx += 8
            T0 = grp[0][0]
            gs = sum(n for _, n in grp)
            vop(lambda e, t0=T0, n=gs: e.tensor_copy(
                VB.rearrange("p (t n) -> p t n", t=24, n=9)[:, 0:n, 0:8],
                VFr[:, t0:t0 + n, :]), reads=VFb, writes=(bVB,))
            for (t0, nr) in grp:
                jloc = t0 - T0
                sl = (t0 // 3) % 2
                vop(lambda e, tt=t0, n=nr: e.tensor_copy(
                    Qpad.rearrange("p (a s) -> p a s", a=4, s=32)[:, 0:n, 0:8],
                    QFr[:, tt:tt + n, :]), reads=QFb, writes=(bQpad,))
                vop(lambda e, tt=t0, n=nr: e.tensor_copy(
                    Kpad.rearrange("p (a s) -> p a s", a=4, s=32)[:, 0:n, 0:8],
                    KFr[:, tt:tt + n, :]), reads=KFb, writes=(bKpad,))
                pbq = rotT.take()
                transpose(pbq, Qpad, reads=(bQpad,))
                vop(lambda e, d=QT[sl][:, :], s=P[pbq][:, 0:128]: e.tensor_copy(d, s),
                    reads=(bP[pbq],), writes=(bQT[sl],))
                pbk = rotT.take()
                transpose(pbk, Kpad, reads=(bKpad,))
                vop(lambda e, d=KT[sl][:, :], s=P[pbk][:, 0:128]: e.tensor_copy(d, s),
                    reads=(bP[pbk],), writes=(bKT[sl],))
                pbs = rotS.take()
                for a in range(nr):
                    mm(pbs, P[pbs][0:128, 128 * a:128 * (a + 1)], KT[sl][32 * a:32 * (a + 1), :],
                       QT[sl][32 * a:32 * (a + 1), :], a == 0, a == nr - 1,
                       reads=(bKT[sl], bQT[sl]), first=(a == 0))
                pr.emit("scalar", lambda e, s=P[pbs][:, 0:128 * nr], d=EB[:, 0:128 * nr]:
                        e.activation(d, s, ACT.Exp, scale=RSQ),
                        reads=(bP[pbs],), writes=(bEB,))
                pba = rotA.take()
                for a in range(nr):
                    j = jloc + a
                    mm(pba, P[pba][0:128, 9 * a:9 * a + 9], EB[:, 128 * a:128 * (a + 1)],
                       VB[:, 9 * j:9 * j + 9], a == 0, a == nr - 1,
                       reads=(bEB, bVB), first=(a == 0))
                vop(lambda e, d=S9[:, 9 * jloc:9 * (jloc + nr)], s=P[pba][:, 0:9 * nr]:
                    e.tensor_copy(d, s), reads=(bP[pba],), writes=(bS9,), waw_ok=True)
            vop(lambda e, n=gs: e.reciprocal(
                RZ.rearrange("p (t u) -> p t u", u=1)[:, 0:n, :],
                S9.rearrange("p (t n) -> p t n", t=24, n=9)[:, 0:n, 8:9]),
                reads=(bS9,), writes=(bRZ,))
            vop(lambda e, n=gs: e.tensor_tensor(
                G24.rearrange("p (t g) -> p t g", t=24, g=8)[:, 0:n, :],
                S9.rearrange("p (t n) -> p t n", t=24, n=9)[:, 0:n, 0:8],
                RZ.rearrange("p (t u) -> p t u", u=1)[:, 0:n, :].broadcast_to((128, n, 8)),
                op=ALU.mult), reads=(bS9, bRZ), writes=(bG24,))
            for hh in range(NKT):
                gg = hh // 4
                bsl = BB[:, TB * hh + T0: TB * hh + T0 + gs].rearrange("p (t u) -> p t u", u=1)
                vop(lambda e, d=bsl, g2=gg, n=gs: e.tensor_tensor(
                    d, d, G24.rearrange("p (t g) -> p t g", t=24, g=8)[:, 0:n, g2:g2 + 1],
                    op=ALU.add), reads=(bBB[hh], bG24), writes=(bBB[hh],))
        ln_stats(BB, bBB, (AA[:, 0:512], AA[:, 512:1024]), (bAA[0], bAA[1]), track_M=True)
        dmacp("gpsimd", "d_str", stats_s[4 + b:5 + b, :], rowS,
              reads=(bMISC,), writes=(bstats[4 + b],), chain=True)
        dmacp("gpsimd", "d_str", stats_s[8 + b:9 + b, :], rowM,
              reads=(bMISC,), writes=(bstats[8 + b],), chain=True)
        dots2_block(BB, bBB, GACC2[:], bGACC2)
        for q in range(4):
            dmacp("gpsimd", "d_xcs", x2c_s[b][:, 4096 * q:4096 * (q + 1)], BB[:, 4096 * q:4096 * (q + 1)],
                  reads=tuple(bBB[8 * q + i] for i in range(8)), writes=(bx2c[b],),
                  waw_ok=(q > 0), chain=True)

    allreduce(GACC2[:], bGACC2, cc_in2, bcc2i, cc_out2, bcc2o)

    for b in range(nb):
        ltok = None
        for q in range(4):
            ltok = dmacp("gpsimd", "d_aal", AA[:, 4096 * q:4096 * (q + 1)],
                         x2c_s[b][:, 4096 * q:4096 * (q + 1)],
                         reads=(bx2c[b],), writes=tuple(bAA[8 * q + i] for i in range(8)), waw_ok=True)
        for bb_ in bAA:
            bb_.w = ltok
        dmacp("gpsimd", "d_str", rowS, stats_s[4 + b:5 + b, :],
              reads=(bstats[4 + b],), writes=(bMISC,), chain=True)
        bcast_row(SBC[:], bSBC, rowS)
        ffn(AA, bAA, BB, bBB, MC2, bMC2, BM2, bBM2, defer_mask=(b == 0))
        if b == 0:
            build_mask(cc_out2, bcc2o, MC2, bMC2, BM2, bBM2)
            for m in range(NKT):
                mask_one(m, BB, bBB, MC2, bMC2, BM2, bBM2)
        resid_add(BB, bBB, AA, bAA)
        dmacp("gpsimd", "d_str", rowM, stats_s[8 + b:9 + b, :],
              reads=(bstats[8 + b],), writes=(bMISC,), chain=True)
        for w in range(4):
            row_to_col(SEG_M, w, M2COL[:], bM2COL)
            for m in range(NKT):
                pb = rotT.take()
                transpose(pb, BB[:, TB * m + 128 * w: TB * m + 128 * (w + 1)], reads=(bBB[m],))
                dst = AA[:, 2048 * (m // 16) + 128 * (m % 16): 2048 * (m // 16) + 128 * (m % 16) + 128]
                db = bAA[m // 4]
                vop(lambda e, d=dst, s=P[pb][:, 0:128]: e.tensor_scalar(
                    d, s, scalar1=M2COL[:, 0:1], scalar2=None, op0=ALU.add),
                    reads=(bP[pb], bM2COL), writes=(db,), waw_ok=True)
            rows0 = TB * b + 128 * w
            dmacp("gpsimd", "d_y0", y_out[rows0:rows0 + 128, 0:2048], AA[:, 0:2048],
                  reads=tuple(bAA[i] for i in range(4)), writes=())
            dmacp("gpsimd", "d_y1", y_out[rows0:rows0 + 128, 2048:4096], AA[:, 2048:4096],
                  reads=tuple(bAA[4 + i] for i in range(4)), writes=())

    semnames = sorted(pr.cnt.keys())
    sems = {s: es.enter_context(nc.semaphore("sem_" + s)) for s in semnames}
    block = es.enter_context(nc.Block())

    def mk(engname):
        def body(eng):
            for (fn, waits, sem, amt) in pr.ops[engname]:
                for s in sorted(waits):
                    eng.wait_ge(sems[s], waits[s])
                r = fn(eng)
                r.then_inc(sems[sem], amt)
            if engname == "scalar":
                eng.wait_ge(sems["d_y0"], pr.cnt["d_y0"])
                eng.wait_ge(sems["d_y1"], pr.cnt["d_y1"])
        return body

    block.tensor(mk("tensor"))
    block.vector(mk("vector"))
    block.scalar(mk("scalar"))
    block.sync(mk("sync"))
    block.gpsimd(mk("gpsimd"))
    es.close()
    return nc


def kernel(L0_attn_x, L0_attn_Wq, L0_attn_bq, L0_attn_Wk, L0_attn_bk, L0_attn_Wv, L0_attn_bv,
           L0_ffn_weight, L0_ffn_bias, L0_ffn_centroids, L0_ffn_assignments):
    x = np.asarray(L0_attn_x, dtype=np.float32)
    B, S, E = x.shape
    ntok_all = B * S
    n_cores = NCORES
    ntok = ntok_all // n_cores
    nb = ntok // TB

    f = np.arange(NF)
    pi = (f % 128) * 32 + (f // 128)

    x2 = np.ascontiguousarray(x.reshape(ntok_all, NF)[:, pi])
    W = np.asarray(L0_ffn_weight, dtype=np.float32)
    Wp = W[np.ix_(pi, pi)]
    wffn = np.ascontiguousarray(Wp.reshape(32, 128, 32, 128).transpose(0, 3, 2, 1))
    biasc = np.ascontiguousarray(np.asarray(L0_ffn_bias, np.float32)[pi].reshape(32, 128).T)
    asg = np.ascontiguousarray(np.asarray(L0_ffn_assignments, np.int32)[pi].reshape(1, NF))
    cin = np.ascontiguousarray(np.asarray(L0_ffn_centroids, np.float32)[:, pi].T
                               .reshape(32, 128, 64).transpose(1, 0, 2))
    Wq = np.asarray(L0_attn_Wq, np.float32)[pi]
    wq = np.ascontiguousarray(Wq.transpose(2, 0, 1).reshape(32, 32, 128, 128).transpose(0, 2, 1, 3))
    bq = np.ascontiguousarray(np.asarray(L0_attn_bq, np.float32).T)
    Wk = np.asarray(L0_attn_Wk, np.float32)[pi]
    wk = np.ascontiguousarray(Wk.transpose(2, 0, 1).reshape(8, 32, 128, 128).transpose(0, 2, 1, 3))
    bk = np.ascontiguousarray(np.asarray(L0_attn_bk, np.float32).T.reshape(1, 1024))
    Wv = np.asarray(L0_attn_Wv, np.float32)[pi]
    wv = np.ascontiguousarray(Wv.transpose(2, 0, 1).reshape(8, 32, 128, 128).transpose(0, 2, 1, 3))
    bv = np.ascontiguousarray(np.asarray(L0_attn_bv, np.float32).T.reshape(1, 1024))
    eye = np.eye(128, dtype=np.float32)
    iota = np.arange(64, dtype=np.float32).reshape(64, 1)

    nc = build_nc(n_cores=n_cores, nb=nb)
    shared = {"wffn": wffn, "biasc": biasc, "asg": asg, "cin": cin, "wq": wq, "bq": bq,
              "wk": wk, "bk": bk, "wv": wv, "bv": bv, "eye": eye, "iota": iota}
    in_maps = []
    for c in range(n_cores):
        d = dict(shared)
        d["x"] = np.ascontiguousarray(x2[ntok * c:ntok * (c + 1)])
        in_maps.append(d)
    res = run_bass_kernel_spmd(nc, in_maps, core_ids=list(range(n_cores)))
    y_int = np.concatenate([np.asarray(res.results[c]["y"]) for c in range(n_cores)], axis=0)
    y = np.empty((ntok_all, NF), dtype=np.float32)
    y[:, pi] = y_int
    return y.reshape(B, S, E)


# version 06 (final)
# speedup vs baseline: 1.0364x; 1.0223x over previous
import math
import numpy as np
from contextlib import ExitStack

import concourse.bass as bass
import concourse.mybir as mybir
from concourse.bass_utils import run_bass_kernel_spmd

F32 = mybir.dt.float32
I32 = mybir.dt.int32
ALU = mybir.AluOpType
ACT = mybir.ActivationFunctionType
AXL = mybir.AxisListType

NCORES = 8
TB = 512
NF = 4096
NKT = 32
EPS = 1e-5
TEMP_INV = 10.0
THR = 0.01
GINIT = -1e30
RSQ = 1.0 / math.sqrt(128.0)

ENGS = ("tensor", "vector", "scalar", "sync", "gpsimd")


class Buf:
    __slots__ = ("w", "rs")

    def __init__(self):
        self.w = None
        self.rs = []


class Prog:
    def __init__(self):
        self.ops = {e: [] for e in ENGS}
        self.cnt = {}
        self.floor = {e: {} for e in ENGS}
        self.last_tok = {}

    def emit(self, eng, fn, reads=(), writes=(), waw_ok=False, dma=None, chain=False, extra=()):
        toks = list(extra)
        for b in reads:
            if b.w is not None:
                toks.append(b.w)
        for b in writes:
            if b.w is not None and not waw_ok:
                toks.append(b.w)
            toks.extend(b.rs)
        if dma is not None and chain and dma in self.last_tok:
            toks.append(self.last_tok[dma])
        waits = {}
        for (s, v) in toks:
            if v > waits.get(s, 0):
                waits[s] = v
        fl = self.floor[eng]
        w2 = {}
        for s, v in waits.items():
            if v > fl.get(s, 0):
                w2[s] = v
                fl[s] = v
        sem = dma if dma is not None else eng
        amt = 16 if dma is not None else 1
        self.cnt[sem] = self.cnt.get(sem, 0) + amt
        tok = (sem, self.cnt[sem])
        self.ops[eng].append((fn, w2, sem, amt))
        if dma is not None:
            self.last_tok[dma] = tok
        for b in reads:
            b.rs.append(tok)
        for b in writes:
            b.w = tok
            b.rs = []
        return tok


def build_nc(n_cores=NCORES, nb=4):
    nc = bass.Bass("TRN2", target_bir_lowering=False)
    ntok = nb * TB

    x_in = nc.dram_tensor("x", [ntok, NF], F32, kind="ExternalInput").ap()
    wffn = nc.dram_tensor("wffn", [32, 128, 32, 128], F32, kind="ExternalInput").ap()
    biasc_in = nc.dram_tensor("biasc", [128, 32], F32, kind="ExternalInput").ap()
    asg_in = nc.dram_tensor("asg", [1, NF], I32, kind="ExternalInput").ap()
    cin_in = nc.dram_tensor("cin", [128, 32, 64], F32, kind="ExternalInput").ap()
    wq_in = nc.dram_tensor("wq", [32, 128, 32, 128], F32, kind="ExternalInput").ap()
    bq_in = nc.dram_tensor("bq", [32, 128], F32, kind="ExternalInput").ap()
    wk_in = nc.dram_tensor("wk", [8, 128, 32, 128], F32, kind="ExternalInput").ap()
    bk_in = nc.dram_tensor("bk", [1, 1024], F32, kind="ExternalInput").ap()
    wv_in = nc.dram_tensor("wv", [8, 128, 32, 128], F32, kind="ExternalInput").ap()
    bv_in = nc.dram_tensor("bv", [1, 1024], F32, kind="ExternalInput").ap()
    eye_in = nc.dram_tensor("eye", [128, 128], F32, kind="ExternalInput").ap()
    iota_in = nc.dram_tensor("iota", [64, 1], F32, kind="ExternalInput").ap()
    y_out = nc.dram_tensor("y", [ntok, NF], F32, kind="ExternalOutput").ap()

    xc_s = nc.dram_tensor("xc_s", [nb, 128, NKT * TB], F32).ap()
    x2c_s = nc.dram_tensor("x2c_s", [nb, 128, NKT * TB], F32).ap()
    wqg_s = nc.dram_tensor("wqg_s", [8, 128, 4096], F32).ap()
    stats_s = nc.dram_tensor("stats_s", [16, TB], F32).ap()
    asgf_s = nc.dram_tensor("asgf_s", [1, NF], F32).ap()
    cc_in1 = nc.dram_tensor("cc_in1", [1, 64], F32).ap()
    cc_out1 = nc.dram_tensor("cc_out1", [1, 64], F32).ap()
    cc_in2 = nc.dram_tensor("cc_in2", [1, 64], F32).ap()
    cc_out2 = nc.dram_tensor("cc_out2", [1, 64], F32).ap()

    es = ExitStack()
    sb = lambda name, shape, dt=F32: es.enter_context(nc.sbuf_tensor(name, shape, dt))

    AA = sb("AA", [128, NKT * TB])
    BB = sb("BB", [128, NKT * TB])
    WSa = sb("WSa", [128, 2048])
    WSb = sb("WSb", [128, 2048])
    CIN = sb("CIN", [128, 2048])
    SBC = sb("SBC", [128, TB])
    MBC = sb("MBC", [128, TB])
    MISC = sb("MISC", [128, 2048])
    EYE = sb("EYE", [128, 128])
    MC1 = sb("MC1", [128, 32])
    MC2 = sb("MC2", [128, 32])
    BM1 = sb("BM1", [128, 32])
    BM2 = sb("BM2", [128, 32])
    BIASC = sb("BIASC", [128, 32])
    BQG = sb("BQG", [1, 1024])
    BK = sb("BK", [1, 1024])
    BV = sb("BV", [1, 1024])
    IOTA = sb("IOTA", [64, 1])
    ONESC = sb("ONESC", [128, 1])
    ONES128 = sb("ONES128", [128, 128])
    GPB = sb("GPB", [64, 128])
    ONESR = sb("ONESR", [1, TB])
    INVS = sb("INVS", [1, TB])
    GACC1 = sb("GACC1", [128, 64])
    GACC2 = sb("GACC2", [128, 64])
    SD = sb("SD", [128, 64])
    EV = sb("EV", [128, 64])
    MRG = sb("MRG", [128, 64])
    MX = sb("MX", [128, 1])
    NMX = sb("NMX", [128, 1])
    ZC = sb("ZC", [128, 1])
    ZZ = sb("ZZ", [128, 1])
    SCOL = sb("SCOL", [128, 1])
    M2COL = sb("M2COL", [128, 1])
    GCOL = sb("GCOL", [64, 1])
    GPOS = sb("GPOS", [64, 1])
    GRED = sb("GRED", [1, 64])
    ASGI = sb("ASGI", [1, 2048], I32)

    AOF = 12288
    Qpad = [AA[:, AOF + 128 * i:AOF + 128 * (i + 1)] for i in range(2)]
    Kpad = [AA[:, AOF + 256 + 128 * i:AOF + 384 + 128 * i] for i in range(2)]
    QT = [AA[:, AOF + 512 + 128 * i:AOF + 640 + 128 * i] for i in range(8)]
    KT = [AA[:, AOF + 1536 + 128 * i:AOF + 1664 + 128 * i] for i in range(8)]
    EB = [AA[:, AOF + 2560 + 384 * i:AOF + 2944 + 384 * i] for i in range(4)]
    VB = sb("VB", [128, 216])
    S9 = sb("S9", [128, 216])
    G24 = sb("G24", [128, 192])
    RZ = sb("RZ", [128, 24])

    P = [es.enter_context(nc.psum_tensor("P%d" % i, [128, 512], F32)) for i in range(8)]

    bAA = [Buf() for _ in range(NKT)]
    bBB = [Buf() for _ in range(NKT)]
    bWSa, bWSb = Buf(), Buf()
    bCIN = Buf()
    bSBC, bMBC, bMISC = Buf(), Buf(), Buf()
    bEYE = Buf()
    bMC1, bMC2, bBM1, bBM2, bBIASC = Buf(), Buf(), Buf(), Buf(), Buf()
    bBQG, bBK, bBV, bIOTA = Buf(), Buf(), Buf(), Buf()
    bONESC, bONESR, bINVS, bONES128, bGPB = Buf(), Buf(), Buf(), Buf(), Buf()
    bGACC1, bGACC2 = Buf(), Buf()
    bSD, bEV, bMRG, bMX, bNMX, bZC, bZZ = (Buf() for _ in range(7))
    bSCOL, bM2COL, bGCOL, bGPOS, bGRED = (Buf() for _ in range(5))
    bQpad = [Buf(), Buf()]
    bKpad = [Buf(), Buf()]
    bQT = [Buf() for _ in range(8)]
    bKT = [Buf() for _ in range(8)]
    bEB = [Buf() for _ in range(4)]
    bVB, bS9, bG24, bRZ = (Buf() for _ in range(4))
    bP = [Buf() for _ in range(8)]
    bxc = [Buf() for _ in range(nb)]
    bx2c = [Buf() for _ in range(nb)]
    bwqg = [Buf() for _ in range(8)]
    bstats = [Buf() for _ in range(16)]
    basgf, bASGI = Buf(), Buf()
    bcc1i, bcc1o, bcc2i, bcc2o = Buf(), Buf(), Buf(), Buf()

    pr = Prog()

    class Rot:
        def __init__(self, a, b):
            self.pair = (a, b)
            self.i = 0

        def take(self):
            j = self.pair[self.i]
            self.i ^= 1
            return j

    SEG_S, SEG_MF, SEG_T, SEG_M = 0, 512, 1024, 1536
    rowS = MISC[0:1, 0:512]
    rowMF = MISC[0:1, 512:1024]
    rowT = MISC[0:1, 1024:1536]
    rowM = MISC[0:1, 1536:2048]

    rotG = Rot(0, 1)
    rotS = Rot(2, 3)
    rotA = Rot(4, 5)
    rotT = Rot(6, 7)

    def dmacp(eng, ch, out_ap, in_ap, reads=(), writes=(), waw_ok=False, accum=False,
              chain=False, extra=()):
        if accum:
            fn = lambda e, o=out_ap, i=in_ap: e.dma_start(out=o, in_=i, accum_op=ALU.add)
        else:
            fn = lambda e, o=out_ap, i=in_ap: e.dma_start(out=o, in_=i)
        return pr.emit(eng, fn, reads=reads, writes=writes, waw_ok=waw_ok, dma=ch,
                       chain=chain, extra=extra)

    def vop(fn, reads=(), writes=(), eng="vector", waw_ok=False):
        return pr.emit(eng, fn, reads=reads, writes=writes, waw_ok=waw_ok)

    def mm(pb, out_ap, lhsT, rhs, start, stop, reads=(), first=False, loose=False):
        fn = lambda e, o=out_ap, l=lhsT, r=rhs, st=start, sp=stop: e.matmul(o, l, r, start=st, stop=sp)
        if first:
            return pr.emit("tensor", fn, reads=reads, writes=(bP[pb],))
        if loose:
            tok = pr.emit("tensor", fn, reads=reads)
            bP[pb].w = tok
            return tok
        return pr.emit("tensor", fn, reads=tuple(reads) + (bP[pb],), writes=(bP[pb],), waw_ok=True)

    def transpose(pb, in_ap, reads):
        fn = lambda e, o=P[pb][:, 0:128], i=in_ap: e.transpose(o, i, EYE[:])
        return pr.emit("tensor", fn, reads=tuple(reads) + (bEYE,), writes=(bP[pb],))

    def bcast_row(dst_sb, dst_buf, src_ap):
        pb = rotT.take()
        mm(pb, P[pb][0:128, 0:TB], ONESR[0:1, 0:128], src_ap, True, True,
           reads=(bONESR, bMISC), first=True)
        vop(lambda e, d=dst_sb, s=P[pb][0:128, 0:TB]: e.tensor_copy(d, s),
            reads=(bP[pb],), writes=(dst_buf,))

    def row_to_col(seg, win, dst_ap, dst_buf):
        pb = rotT.take()
        transpose(pb, MISC[:, seg + 128 * win:seg + 128 * (win + 1)], reads=(bMISC,))
        vop(lambda e, d=dst_ap, s=P[pb][:, 0:1]: e.tensor_copy(d, s),
            reads=(bP[pb],), writes=(dst_buf,))

    def colsum_pass(tile_ap, tile_bufs, out_row, scale, square=False, sq_ap=None, sq_bufs=None,
                    bc_dst=None):
        pb = rotG.take()
        for k in range(NKT):
            if square:
                t = k % 2
                vop(lambda e, d=sq_ap[t], s=tile_ap(k): e.tensor_tensor(d, s, s, op=ALU.mult),
                    reads=(tile_bufs[k],), writes=(sq_bufs[t],))
                mm(pb, P[pb][0:128, 0:TB], ONES128[:], sq_ap[t], k == 0, k == NKT - 1,
                   reads=(bONES128, sq_bufs[t]), first=(k == 0))
            else:
                mm(pb, P[pb][0:128, 0:TB], ONES128[:], tile_ap(k), k == 0, k == NKT - 1,
                   reads=(bONES128, tile_bufs[k]), first=(k == 0))
        vop(lambda e, d=MISC[0:1, out_row:out_row + TB], s=P[pb][0:1, 0:TB], sc=scale:
            e.tensor_scalar(d, s, scalar1=sc, scalar2=None, op0=ALU.mult),
            reads=(bP[pb],), writes=(bMISC,))
        if bc_dst is not None:
            dst, dbuf = bc_dst
            vop(lambda e, d=dst, s=P[pb][0:128, 0:TB], sc=scale:
                e.tensor_scalar(d, s, scalar1=sc, scalar2=None, op0=ALU.mult),
                reads=(bP[pb],), writes=(dbuf,))

    dmacp("gpsimd", "d_msc", EYE[:], eye_in, writes=(bEYE,), chain=True)
    dmacp("gpsimd", "d_msc", IOTA[:], iota_in, writes=(bIOTA,), chain=True)
    dmacp("gpsimd", "d_msc", BIASC[:], biasc_in, writes=(bBIASC,), chain=True)
    dmacp("gpsimd", "d_msc", BK[:], bk_in, writes=(bBK,), chain=True)
    dmacp("gpsimd", "d_msc", BV[:], bv_in, writes=(bBV,), chain=True)

    dmacp("sync", "d_cin", CIN[:], cin_in.rearrange("p k c -> p (k c)"), writes=(bCIN,), chain=True)
    for g in range(8):
        for j in range(4):
            dmacp("gpsimd", "d_msc", BQG[0:1, 128 * g:128 * (g + 1)],
                  bq_in[4 * g + j:4 * g + j + 1, :],
                  writes=(bBQG,), waw_ok=(g > 0 or j > 0), accum=(j > 0), chain=True)
    for h in range(2):
        dst = (WSa if h == 0 else WSb)[0:1, 0:2048]
        bufd = bWSa if h == 0 else bWSb
        dmacp("gpsimd", "d_msc", ASGI[:], asg_in[0:1, 2048 * h:2048 * (h + 1)],
              writes=(bASGI,), chain=True)
        vop(lambda e, d=dst, s=ASGI[:]: e.tensor_copy(d, s),
            reads=(bASGI,), writes=(bufd,))
        dmacp("gpsimd", "d_msc", asgf_s[0:1, 2048 * h:2048 * (h + 1)], dst,
              reads=(bufd,), writes=(basgf,), waw_ok=(h == 1), chain=True)
    vop(lambda e: e.memset(MISC[:], 0.0), writes=(bMISC,))
    vop(lambda e: e.memset(ONESC[:], 1.0), writes=(bONESC,))
    vop(lambda e: e.memset(ONES128[:], 1.0), writes=(bONES128,))
    vop(lambda e: e.memset(ONESR[:], 1.0), writes=(bONESR,))
    vop(lambda e: e.memset(GACC1[:], GINIT), writes=(bGACC1,))
    vop(lambda e: e.memset(GACC2[:], GINIT), writes=(bGACC2,))
    for g in range(8):
        for half in range(2):
            sl = slice(16 * half, 16 * (half + 1))
            dmacp("sync", "d_wqb", WSa[:, :],
                  wq_in[4 * g + 0][:, sl, :].rearrange("p k d -> p (k d)"),
                  writes=(bWSa,), chain=True)
            for j in range(1, 4):
                dmacp("sync", "d_wqb", WSb[:, :],
                      wq_in[4 * g + j][:, sl, :].rearrange("p k d -> p (k d)"),
                      writes=(bWSb,), chain=True)
                vop(lambda e: e.tensor_tensor(WSa[:, :], WSa[:, :], WSb[:, :], op=ALU.add),
                    reads=(bWSa, bWSb), writes=(bWSa,))
            dmacp("sync", "d_wqb", wqg_s[g][:, 2048 * half:2048 * (half + 1)], WSa[:, :],
                  reads=(bWSa,), writes=(bwqg[g],), waw_ok=(half == 1), chain=True)

    arenas = [(AA, bAA), (BB, bBB)]
    for b in range(nb):
        AR, bAR = arenas[b % 2]
        OT, bOT = arenas[1 - b % 2]
        for st in range(4):
            slot = st % 2
            stg = OT[:, 4096 * slot:4096 * (slot + 1)]
            sbufs = tuple(bOT[8 * slot + i] for i in range(8))
            dmacp("gpsimd", "d_stg%d" % slot, stg,
                  x_in[TB * b + 128 * st: TB * b + 128 * (st + 1), :],
                  writes=sbufs, waw_ok=True)
            for k in range(NKT):
                pb = rotT.take()
                transpose(pb, stg[:, 128 * k:128 * (k + 1)], reads=sbufs)
                vop(lambda e, d=AR[:, TB * k + 128 * st: TB * k + 128 * (st + 1)],
                    s=P[pb][:, 0:128]: e.tensor_copy(d, s),
                    reads=(bP[pb],), writes=(bAR[k],), waw_ok=True)
        colsum_pass(lambda k: AR[:, TB * k:TB * (k + 1)], bAR, SEG_MF, 1.0 / NF,
                    bc_dst=(MBC[:], bMBC))
        sq_ap = (OT[:, 8192:8704], OT[:, 8704:9216])
        sq_bufs = (bOT[16], bOT[17])
        colsum_pass(lambda k: AR[:, TB * k:TB * (k + 1)], bAR, SEG_T, 1.0 / NF,
                    square=True, sq_ap=sq_ap, sq_bufs=sq_bufs)
        vop(lambda e: e.tensor_tensor(INVS[:], rowMF, rowMF, op=ALU.mult),
            reads=(bMISC,), writes=(bINVS,))
        vop(lambda e: e.tensor_tensor(rowT, rowT, INVS[:], op=ALU.subtract),
            reads=(bMISC, bINVS), writes=(bMISC,))
        vop(lambda e: e.tensor_scalar(rowT, rowT, scalar1=EPS, scalar2=None, op0=ALU.add),
            reads=(bMISC,), writes=(bMISC,))
        vop(lambda e: e.reciprocal(INVS[:], rowT), reads=(bMISC,), writes=(bINVS,))
        pr.emit("scalar", lambda e: e.activation(rowS, INVS[:], ACT.Sqrt),
                reads=(bINVS,), writes=(bMISC,))
        dmacp("gpsimd", "d_str", stats_s[8 + b:9 + b, :], rowS,
              reads=(bMISC,), writes=(bstats[8 + b],), chain=True)
        dmacp("gpsimd", "d_str", stats_s[0 + b:1 + b, :], rowMF,
              reads=(bMISC,), writes=(bstats[0 + b],), chain=True)
        for k in range(NKT):
            vop(lambda e, t=AR[:, TB * k:TB * (k + 1)]: e.tensor_tensor(t, t, MBC[:], op=ALU.subtract),
                reads=(bAR[k], bMBC), writes=(bAR[k],))

        def dots_block(arena, barena, gacc_ap, bgacc):
            for w in range(4):
                row_to_col(SEG_S, w, SCOL[:], bSCOL)
                pb = rotG.take()
                for k in range(NKT):
                    mm(pb, P[pb][0:128, 0:64], arena[:, TB * k + 128 * w: TB * k + 128 * (w + 1)],
                       CIN[:, 64 * k:64 * (k + 1)], k == 0, k == NKT - 1,
                       reads=(barena[k], bCIN), first=(k == 0), loose=True)
                vop(lambda e, s=P[pb][0:128, 0:64]: e.tensor_scalar(
                    SD[:], s, scalar1=SCOL[:, 0:1], scalar2=TEMP_INV, op0=ALU.mult, op1=ALU.mult),
                    reads=(bP[pb], bSCOL), writes=(bSD,))
                vop(lambda e: e.reduce_max(MX[:], SD[:], axis=AXL.X), reads=(bSD,), writes=(bMX,))
                vop(lambda e: e.tensor_scalar(NMX[:], MX[:], scalar1=-1.0, scalar2=None, op0=ALU.mult),
                    reads=(bMX,), writes=(bNMX,))
                pr.emit("scalar", lambda e: e.activation(EV[:], SD[:], ACT.Exp, bias=NMX[:, 0:1],
                        accum_out=ZC[:]), reads=(bSD, bNMX), writes=(bEV, bZC))
                vop(lambda e: e.tensor_scalar(ZZ[:], ZC[:], scalar1=THR, scalar2=None, op0=ALU.mult),
                    reads=(bZC,), writes=(bZZ,))
                vop(lambda e: e.tensor_scalar(MRG[:], EV[:], scalar1=ZZ[:, 0:1], scalar2=None,
                    op0=ALU.subtract), reads=(bEV, bZZ), writes=(bMRG,))
                vop(lambda e, g=gacc_ap: e.tensor_tensor(g, g, MRG[:], op=ALU.max),
                    reads=(bMRG, bgacc), writes=(bgacc,))

        dots_block(AR, bAR, GACC1[:], bGACC1)
        for q in range(4):
            dmacp("gpsimd", "d_xcs", xc_s[b][:, 4096 * q:4096 * (q + 1)], AR[:, 4096 * q:4096 * (q + 1)],
                  reads=tuple(bAR[8 * q + i] for i in range(8)), writes=(bxc[b],),
                  waw_ok=(q > 0), chain=True)

    def allreduce(gacc_ap, bgacc, cci, bcci, cco, bcco):
        pr.emit("gpsimd", lambda e, g=gacc_ap: e.tensor_reduce(GRED[:], g, axis=AXL.C, op=ALU.max),
                reads=(bgacc,), writes=(bGRED,))
        dmacp("gpsimd", "d_cc", cci, GRED[:], reads=(bGRED,), writes=(bcci,), chain=True)
        pr.emit("gpsimd", lambda e, i=cci, o=cco: e.collective_compute(
            "AllReduce", ALU.max, replica_groups=[list(range(n_cores))],
            ins=[i.opt()], outs=[o.opt()]),
            reads=(bcci,), writes=(bcco,))

    allreduce(GACC1[:], bGACC1, cc_in1, bcc1i, cc_out1, bcc1o)

    def build_mask(cco, bcco, MC, bMC, BM, bBM):
        dmacp("gpsimd", "d_msc", GCOL[:], cco.rearrange("u c -> c u"),
              reads=(bcco,), writes=(bGCOL,), chain=True)
        vop(lambda e: e.tensor_scalar(GPOS[:], GCOL[:], scalar1=0.0, scalar2=0.5,
            op0=ALU.is_gt, op1=ALU.mult), reads=(bGCOL,), writes=(bGPOS,))
        vop(lambda e: e.tensor_copy(GPB[:], GPOS[:].broadcast_to((64, 128))),
            reads=(bGPOS,), writes=(bGPB,))
        for h in range(2):
            W, bW = (WSa, bWSa) if h == 0 else (WSb, bWSb)
            dmacp("gpsimd", "d_msc", W[0:64, :],
                  asgf_s[0:1, 2048 * h:2048 * (h + 1)].broadcast_to((64, 2048)),
                  reads=(basgf,), writes=(bW,), chain=True)
            vop(lambda e, w=W: e.tensor_scalar(w[0:64, :], w[0:64, :], scalar1=IOTA[:, 0:1],
                scalar2=None, op0=ALU.is_equal), reads=(bW, bIOTA), writes=(bW,))
            for jj in range(4):
                pb = rotG.take()
                mm(pb, P[pb][0:128, 0:512], GPB[:], W[0:64, 512 * jj:512 * (jj + 1)], True, True,
                   reads=(bGPB, bW), first=True)
                vop(lambda e, s=P[pb][0:1, 0:512]: e.tensor_copy(rowT, s),
                    reads=(bP[pb],), writes=(bMISC,))
                for t in range(4):
                    m = 16 * h + 4 * jj + t
                    pb2 = rotT.take()
                    transpose(pb2, MISC[:, SEG_T + 128 * t:SEG_T + 128 * (t + 1)], reads=(bMISC,))
                    vop(lambda e, d=MC[:, m:m + 1], s=P[pb2][:, 0:1]: e.tensor_copy(d, s),
                        reads=(bP[pb2],), writes=(bMC,), waw_ok=True)
        vop(lambda e, d=BM[:], mc=MC[:]: e.tensor_tensor(d, BIASC[:], mc, op=ALU.mult),
            reads=(bBIASC, bMC), writes=(bBM,))

    wq_eng = {"d_w0": "sync", "d_w1": "scalar"}
    wq_names = ["d_w0", "d_w1"]
    wq_i = [0]

    def gemm_slab(srcs, rhs_ap, rhs_bufs, pb, n_out, bias_mm=None):
        slots = ((WSa, bWSa), (WSb, bWSb))
        for h in range(2):
            ch = wq_names[wq_i[0] % 2]
            wq_i[0] += 1
            dmacp(wq_eng[ch], ch, slots[h][0][:, :], srcs[h][0],
                  reads=srcs[h][1], writes=(slots[h][1],))
        for k in range(NKT):
            dst, bufd = slots[k // 16]
            lhs = dst[:, 128 * (k % 16):128 * (k % 16 + 1)]
            mm(pb, P[pb][0:128, 0:n_out], lhs, rhs_ap(k), k == 0,
               (k == NKT - 1) and bias_mm is None,
               reads=(bufd, rhs_bufs[k]), first=(k == 0), loose=True)
        if bias_mm is not None:
            blhs, brhs, brd = bias_mm
            mm(pb, P[pb][0:128, 0:n_out], blhs, brhs, False, True, reads=brd, loose=True)

    def mask_one(m, out_arena, bout, MC, bMC, BM, bBM):
        vop(lambda e, t=out_arena[:, TB * m:TB * (m + 1)], mm_=m, mc=MC, bm2=BM:
            e.tensor_scalar(t, t, scalar1=mc[:, mm_:mm_ + 1], scalar2=bm2[:, mm_:mm_ + 1],
                            op0=ALU.mult, op1=ALU.add),
            reads=(bout[m], bMC, bBM), writes=(bout[m],))

    def ffn(rhs_arena, brhs, out_arena, bout, MC, bMC, BM, bBM, defer_mask):
        for m in range(NKT):
            pb = rotG.take()
            gemm_slab([(wffn[m, :, 0:16, :].rearrange("p k o -> p (k o)"), ()),
                       (wffn[m, :, 16:32, :].rearrange("p k o -> p (k o)"), ())],
                      lambda k: rhs_arena[:, TB * k:TB * (k + 1)], brhs, pb, TB)
            vop(lambda e, d=out_arena[:, TB * m:TB * (m + 1)], s=P[pb][0:128, 0:TB]:
                e.tensor_tensor(d, s, SBC[:], op=ALU.mult),
                reads=(bP[pb], bSBC), writes=(bout[m],))
            if not defer_mask:
                mask_one(m, out_arena, bout, MC, bMC, BM, bBM)

    def resid_add(dst_arena, bdst, src_arena, bsrc):
        for m in range(NKT):
            vop(lambda e, d=dst_arena[:, TB * m:TB * (m + 1)], s=src_arena[:, TB * m:TB * (m + 1)]:
                e.tensor_tensor(d, d, s, op=ALU.add),
                reads=(bdst[m], bsrc[m]), writes=(bdst[m],))

    def ln_stats(arena, barena, sq_ap, sq_bufs, track_M):
        colsum_pass(lambda k: arena[:, TB * k:TB * (k + 1)], barena, SEG_MF, 1.0 / NF,
                    bc_dst=(MBC[:], bMBC))
        colsum_pass(lambda k: arena[:, TB * k:TB * (k + 1)], barena, SEG_T, 1.0 / NF,
                    square=True, sq_ap=sq_ap, sq_bufs=sq_bufs)
        vop(lambda e: e.tensor_tensor(INVS[:], rowMF, rowMF, op=ALU.mult),
            reads=(bMISC,), writes=(bINVS,))
        vop(lambda e: e.tensor_tensor(rowT, rowT, INVS[:], op=ALU.subtract),
            reads=(bMISC, bINVS), writes=(bMISC,))
        vop(lambda e: e.tensor_scalar(rowT, rowT, scalar1=EPS, scalar2=None, op0=ALU.add),
            reads=(bMISC,), writes=(bMISC,))
        vop(lambda e: e.reciprocal(INVS[:], rowT), reads=(bMISC,), writes=(bINVS,))
        pr.emit("scalar", lambda e: e.activation(rowS, INVS[:], ACT.Sqrt),
                reads=(bINVS,), writes=(bMISC,))
        pr.emit("scalar", lambda e: e.activation(INVS[:], rowT, ACT.Sqrt),
                reads=(bMISC, bINVS), writes=(bINVS,))
        if track_M:
            vop(lambda e: e.tensor_tensor(rowM, rowM, rowMF, op=ALU.add),
                reads=(bMISC,), writes=(bMISC,))
        for k in range(NKT):
            vop(lambda e, t=arena[:, TB * k:TB * (k + 1)]: e.tensor_tensor(t, t, MBC[:], op=ALU.subtract),
                reads=(barena[k], bMBC), writes=(barena[k],))

    def dots2_block(arena, barena, gacc_ap, bgacc):
        for w in range(4):
            row_to_col(SEG_S, w, SCOL[:], bSCOL)
            pb = rotG.take()
            for k in range(NKT):
                mm(pb, P[pb][0:128, 0:64], arena[:, TB * k + 128 * w: TB * k + 128 * (w + 1)],
                   CIN[:, 64 * k:64 * (k + 1)], k == 0, k == NKT - 1,
                   reads=(barena[k], bCIN), first=(k == 0), loose=True)
            vop(lambda e, s=P[pb][0:128, 0:64]: e.tensor_scalar(
                SD[:], s, scalar1=SCOL[:, 0:1], scalar2=TEMP_INV, op0=ALU.mult, op1=ALU.mult),
                reads=(bP[pb], bSCOL), writes=(bSD,))
            vop(lambda e: e.reduce_max(MX[:], SD[:], axis=AXL.X), reads=(bSD,), writes=(bMX,))
            vop(lambda e: e.tensor_scalar(NMX[:], MX[:], scalar1=-1.0, scalar2=None, op0=ALU.mult),
                reads=(bMX,), writes=(bNMX,))
            pr.emit("scalar", lambda e: e.activation(EV[:], SD[:], ACT.Exp, bias=NMX[:, 0:1],
                    accum_out=ZC[:]), reads=(bSD, bNMX), writes=(bEV, bZC))
            vop(lambda e: e.tensor_scalar(ZZ[:], ZC[:], scalar1=THR, scalar2=None, op0=ALU.mult),
                reads=(bZC,), writes=(bZZ,))
            vop(lambda e: e.tensor_scalar(MRG[:], EV[:], scalar1=ZZ[:, 0:1], scalar2=None,
                op0=ALU.subtract), reads=(bEV, bZZ), writes=(bMRG,))
            vop(lambda e, g=gacc_ap: e.tensor_tensor(g, g, MRG[:], op=ALU.max),
                reads=(bMRG, bgacc), writes=(bgacc,))

    for b in range(nb):
        ltok = None
        for q in range(4):
            ltok = dmacp("gpsimd", "d_aal", AA[:, 4096 * q:4096 * (q + 1)],
                         xc_s[b][:, 4096 * q:4096 * (q + 1)],
                         reads=(bxc[b],), writes=tuple(bAA[8 * q + i] for i in range(8)), waw_ok=True)
        for bb_ in bAA:
            bb_.w = ltok
        dmacp("gpsimd", "d_str", rowS, stats_s[8 + b:9 + b, :],
              reads=(bstats[8 + b],), writes=(bMISC,), chain=True)
        bcast_row(SBC[:], bSBC, rowS)
        ffn(AA, bAA, BB, bBB, MC1, bMC1, BM1, bBM1, defer_mask=(b == 0))
        if b == 0:
            build_mask(cc_out1, bcc1o, MC1, bMC1, BM1, bBM1)
            for m in range(NKT):
                mask_one(m, BB, bBB, MC1, bMC1, BM1, bBM1)
        resid_add(BB, bBB, AA, bAA)
        dmacp("gpsimd", "d_str", rowM, stats_s[0 + b:1 + b, :],
              reads=(bstats[0 + b],), writes=(bMISC,), chain=True)
        ln_stats(BB, bBB, (AA[:, 0:512], AA[:, 512:1024]), (bAA[0], bAA[1]), track_M=True)
        bcast_row(SBC[:], bSBC, rowS)
        for (w_src, w_bufs, bias_sb, bias_buf, foff) in (
                (lambda g: wqg_s[g].rearrange("p f -> p f"), bwqg, BQG, bBQG, 0),
                (lambda g: wk_in[g].rearrange("p k d -> p (k d)"), None, BK, bBK, 8),
                (lambda g: wv_in[g].rearrange("p k d -> p (k d)"), None, BV, bBV, 16)):
            for g in range(8):
                pb = rotG.take()
                src = w_src(g)
                rd = (w_bufs[g],) if w_bufs is not None else ()
                gemm_slab([(src[:, 0:2048], rd), (src[:, 2048:4096], rd)],
                          lambda k: BB[:, TB * k:TB * (k + 1)], bBB, pb, TB,
                          bias_mm=(bias_sb[0:1, 128 * g:128 * (g + 1)], INVS[:],
                                   (bias_buf, bINVS)))
                dst = AA[:, TB * (foff + g):TB * (foff + g + 1)]
                vop(lambda e, d=dst, s=P[pb][0:128, 0:TB]: e.tensor_tensor(d, s, SBC[:], op=ALU.mult),
                    reads=(bP[pb], bSBC), writes=(bAA[foff + g],))
        QFb = tuple(bAA[i] for i in range(8))
        KFb = tuple(bAA[8 + i] for i in range(8))
        VFb = tuple(bAA[16 + i] for i in range(8))
        for i_ in range(2):
            vop(lambda e, i2=i_: e.memset(Qpad[i2], 0.0), writes=(bQpad[i_],))
            vop(lambda e, i2=i_: e.memset(Kpad[i2], 0.0), writes=(bKpad[i_],))
        vop(lambda e: e.memset(VB[:], 1.0), writes=(bVB,))
        QFr = AA[:, 0:4096].rearrange("p (g t) -> p t g", g=8, t=TB)
        KFr = AA[:, 4096:8192].rearrange("p (g t) -> p t g", g=8, t=TB)
        VFr = AA[:, 8192:12288].rearrange("p (g t) -> p t g", g=8, t=TB)
        def av_emit(item, T0):
            gi2, t0, nr, sle = item
            jloc = t0 - T0
            pba = rotA.take()
            for a in range(nr):
                j = jloc + a
                mm(pba, P[pba][0:128, 9 * a:9 * a + 9], EB[sle][:, 128 * a:128 * (a + 1)],
                   VB[:, 9 * j:9 * j + 9], a == 0, a == nr - 1,
                   reads=(bEB[sle], bVB), first=(a == 0))
            vop(lambda e, d=S9[:, 9 * jloc:9 * (jloc + nr)], s=P[pba][:, 0:9 * nr]:
                e.tensor_copy(d, s), reads=(bP[pba],), writes=(bS9,), waw_ok=True)

        triples = [(t0, min(3, TB - t0)) for t0 in range(0, TB, 3)]
        gidx = 0
        while gidx < len(triples):
            grp = triples[gidx:gidx + 8]
            gidx += 8
            T0 = grp[0][0]
            gs = sum(n for _, n in grp)
            vop(lambda e, t0=T0, n=gs: e.tensor_copy(
                VB[:].rearrange("p (t n) -> p t n", t=24, n=9)[:, 0:n, 0:8],
                VFr[:, t0:t0 + n, :]), reads=VFb, writes=(bVB,))
            for gi2, (t0, nr) in enumerate(grp):
                slp = gi2 % 2
                vop(lambda e, tt=t0, n=nr, s2=slp: e.tensor_copy(
                    Qpad[s2].rearrange("p (a s) -> p a s", a=4, s=32)[:, 0:n, 0:8],
                    QFr[:, tt:tt + n, :]), reads=QFb, writes=(bQpad[slp],))
                pbq = rotT.take()
                transpose(pbq, Qpad[slp], reads=(bQpad[slp],))
                vop(lambda e, d=QT[gi2][:, :], s=P[pbq][:, 0:128]: e.tensor_copy(d, s),
                    reads=(bP[pbq],), writes=(bQT[gi2],))
                vop(lambda e, tt=t0, n=nr, s2=slp: e.tensor_copy(
                    Kpad[s2].rearrange("p (a s) -> p a s", a=4, s=32)[:, 0:n, 0:8],
                    KFr[:, tt:tt + n, :]), reads=KFb, writes=(bKpad[slp],))
                pbk = rotT.take()
                transpose(pbk, Kpad[slp], reads=(bKpad[slp],))
                vop(lambda e, d=KT[gi2][:, :], s=P[pbk][:, 0:128]: e.tensor_copy(d, s),
                    reads=(bP[pbk],), writes=(bKT[gi2],))
            pbs_list = []
            for gi2, (t0, nr) in enumerate(grp):
                pbs = rotS.take()
                for a in range(nr):
                    mm(pbs, P[pbs][0:128, 128 * a:128 * (a + 1)], KT[gi2][32 * a:32 * (a + 1), :],
                       QT[gi2][32 * a:32 * (a + 1), :], a == 0, a == nr - 1,
                       reads=(bKT[gi2], bQT[gi2]), first=(a == 0))
                sle = gi2 % 4
                pr.emit("scalar", lambda e, s=P[pbs][:, 0:128 * nr], d=EB[sle][:, 0:128 * nr]:
                        e.activation(d, s, ACT.Exp, scale=RSQ),
                        reads=(bP[pbs],), writes=(bEB[sle],))
                pbs_list.append((gi2, t0, nr, sle))
                if len(pbs_list) >= 2:
                    av_emit(pbs_list.pop(0), T0)
            for item in pbs_list:
                av_emit(item, T0)
            vop(lambda e, n=gs: e.reciprocal(
                RZ[:].rearrange("p (t u) -> p t u", u=1)[:, 0:n, :],
                S9[:].rearrange("p (t n) -> p t n", t=24, n=9)[:, 0:n, 8:9]),
                reads=(bS9,), writes=(bRZ,))
            vop(lambda e, n=gs: e.tensor_tensor(
                G24[:].rearrange("p (t g) -> p t g", t=24, g=8)[:, 0:n, :],
                S9[:].rearrange("p (t n) -> p t n", t=24, n=9)[:, 0:n, 0:8],
                RZ[:].rearrange("p (t u) -> p t u", u=1)[:, 0:n, :].broadcast_to((128, n, 8)),
                op=ALU.mult), reads=(bS9, bRZ), writes=(bG24,))
            for hh in range(NKT):
                gg = hh // 4
                bsl = BB[:, TB * hh + T0: TB * hh + T0 + gs].rearrange("p (t u) -> p t u", u=1)
                vop(lambda e, d=bsl, g2=gg, n=gs: e.tensor_tensor(
                    d, d, G24[:].rearrange("p (t g) -> p t g", t=24, g=8)[:, 0:n, g2:g2 + 1],
                    op=ALU.add), reads=(bBB[hh], bG24), writes=(bBB[hh],))
        ln_stats(BB, bBB, (AA[:, 0:512], AA[:, 512:1024]), (bAA[0], bAA[1]), track_M=True)
        dmacp("gpsimd", "d_str", stats_s[4 + b:5 + b, :], rowS,
              reads=(bMISC,), writes=(bstats[4 + b],), chain=True)
        dmacp("gpsimd", "d_str", stats_s[8 + b:9 + b, :], rowM,
              reads=(bMISC,), writes=(bstats[8 + b],), chain=True)
        dots2_block(BB, bBB, GACC2[:], bGACC2)
        for q in range(4):
            dmacp("gpsimd", "d_xcs", x2c_s[b][:, 4096 * q:4096 * (q + 1)], BB[:, 4096 * q:4096 * (q + 1)],
                  reads=tuple(bBB[8 * q + i] for i in range(8)), writes=(bx2c[b],),
                  waw_ok=(q > 0), chain=True)

    allreduce(GACC2[:], bGACC2, cc_in2, bcc2i, cc_out2, bcc2o)

    for b in range(nb):
        ltok = None
        for q in range(4):
            ltok = dmacp("gpsimd", "d_aal", AA[:, 4096 * q:4096 * (q + 1)],
                         x2c_s[b][:, 4096 * q:4096 * (q + 1)],
                         reads=(bx2c[b],), writes=tuple(bAA[8 * q + i] for i in range(8)), waw_ok=True)
        for bb_ in bAA:
            bb_.w = ltok
        dmacp("gpsimd", "d_str", rowS, stats_s[4 + b:5 + b, :],
              reads=(bstats[4 + b],), writes=(bMISC,), chain=True)
        bcast_row(SBC[:], bSBC, rowS)
        ffn(AA, bAA, BB, bBB, MC2, bMC2, BM2, bBM2, defer_mask=(b == 0))
        if b == 0:
            build_mask(cc_out2, bcc2o, MC2, bMC2, BM2, bBM2)
            for m in range(NKT):
                mask_one(m, BB, bBB, MC2, bMC2, BM2, bBM2)
        resid_add(BB, bBB, AA, bAA)
        dmacp("gpsimd", "d_str", rowM, stats_s[8 + b:9 + b, :],
              reads=(bstats[8 + b],), writes=(bMISC,), chain=True)
        for w in range(4):
            row_to_col(SEG_M, w, M2COL[:], bM2COL)
            for m in range(NKT):
                pb = rotT.take()
                transpose(pb, BB[:, TB * m + 128 * w: TB * m + 128 * (w + 1)], reads=(bBB[m],))
                dst = AA[:, 2048 * (m // 16) + 128 * (m % 16): 2048 * (m // 16) + 128 * (m % 16) + 128]
                db = bAA[m // 4]
                vop(lambda e, d=dst, s=P[pb][:, 0:128]: e.tensor_scalar(
                    d, s, scalar1=M2COL[:, 0:1], scalar2=None, op0=ALU.add),
                    reads=(bP[pb], bM2COL), writes=(db,), waw_ok=True)
            rows0 = TB * b + 128 * w
            dmacp("gpsimd", "d_y0", y_out[rows0:rows0 + 128, 0:2048], AA[:, 0:2048],
                  reads=tuple(bAA[i] for i in range(4)), writes=())
            dmacp("gpsimd", "d_y1", y_out[rows0:rows0 + 128, 2048:4096], AA[:, 2048:4096],
                  reads=tuple(bAA[4 + i] for i in range(4)), writes=())

    semnames = sorted(pr.cnt.keys())
    sems = {s: es.enter_context(nc.semaphore("sem_" + s)) for s in semnames}
    block = es.enter_context(nc.Block())

    def mk(engname):
        def body(eng):
            for (fn, waits, sem, amt) in pr.ops[engname]:
                for s in sorted(waits):
                    eng.wait_ge(sems[s], waits[s])
                r = fn(eng)
                r.then_inc(sems[sem], amt)
            if engname == "scalar":
                eng.wait_ge(sems["d_y0"], pr.cnt["d_y0"])
                eng.wait_ge(sems["d_y1"], pr.cnt["d_y1"])
        return body

    block.tensor(mk("tensor"))
    block.vector(mk("vector"))
    block.scalar(mk("scalar"))
    block.sync(mk("sync"))
    block.gpsimd(mk("gpsimd"))
    es.close()
    return nc


def kernel(L0_attn_x, L0_attn_Wq, L0_attn_bq, L0_attn_Wk, L0_attn_bk, L0_attn_Wv, L0_attn_bv,
           L0_ffn_weight, L0_ffn_bias, L0_ffn_centroids, L0_ffn_assignments):
    x = np.asarray(L0_attn_x, dtype=np.float32)
    B, S, E = x.shape
    ntok_all = B * S
    n_cores = NCORES
    ntok = ntok_all // n_cores
    nb = ntok // TB

    f = np.arange(NF)
    pi = (f % 128) * 32 + (f // 128)

    x2 = np.ascontiguousarray(x.reshape(ntok_all, NF)[:, pi])
    W = np.asarray(L0_ffn_weight, dtype=np.float32)
    Wp = W[np.ix_(pi, pi)]
    wffn = np.ascontiguousarray(Wp.reshape(32, 128, 32, 128).transpose(0, 3, 2, 1))
    biasc = np.ascontiguousarray(np.asarray(L0_ffn_bias, np.float32)[pi].reshape(32, 128).T)
    asg = np.ascontiguousarray(np.asarray(L0_ffn_assignments, np.int32)[pi].reshape(1, NF))
    cin = np.ascontiguousarray(np.asarray(L0_ffn_centroids, np.float32)[:, pi].T
                               .reshape(32, 128, 64).transpose(1, 0, 2))
    Wq = np.asarray(L0_attn_Wq, np.float32)[pi]
    wq = np.ascontiguousarray(Wq.transpose(2, 0, 1).reshape(32, 32, 128, 128).transpose(0, 2, 1, 3))
    bq = np.ascontiguousarray(np.asarray(L0_attn_bq, np.float32).T)
    Wk = np.asarray(L0_attn_Wk, np.float32)[pi]
    wk = np.ascontiguousarray(Wk.transpose(2, 0, 1).reshape(8, 32, 128, 128).transpose(0, 2, 1, 3))
    bk = np.ascontiguousarray(np.asarray(L0_attn_bk, np.float32).T.reshape(1, 1024))
    Wv = np.asarray(L0_attn_Wv, np.float32)[pi]
    wv = np.ascontiguousarray(Wv.transpose(2, 0, 1).reshape(8, 32, 128, 128).transpose(0, 2, 1, 3))
    bv = np.ascontiguousarray(np.asarray(L0_attn_bv, np.float32).T.reshape(1, 1024))
    eye = np.eye(128, dtype=np.float32)
    iota = np.arange(64, dtype=np.float32).reshape(64, 1)

    nc = build_nc(n_cores=n_cores, nb=nb)
    shared = {"wffn": wffn, "biasc": biasc, "asg": asg, "cin": cin, "wq": wq, "bq": bq,
              "wk": wk, "bk": bk, "wv": wv, "bv": bv, "eye": eye, "iota": iota}
    in_maps = []
    for c in range(n_cores):
        d = dict(shared)
        d["x"] = np.ascontiguousarray(x2[ntok * c:ntok * (c + 1)])
        in_maps.append(d)
    res = run_bass_kernel_spmd(nc, in_maps, core_ids=list(range(n_cores)))
    y_int = np.concatenate([np.asarray(res.results[c]["y"]) for c in range(n_cores)], axis=0)
    y = np.empty((ntok_all, NF), dtype=np.float32)
    y[:, pi] = y_int
    return y.reshape(B, S, E)
